# version 25
# baseline (speedup 1.0000x reference)
import numpy as np
import ml_dtypes
from contextlib import ExitStack

import concourse.bass as bass
import concourse.mybir as mybir
from concourse.bass_utils import run_bass_kernel_spmd

F32 = mybir.dt.float32
BF16 = mybir.dt.bfloat16
AF = mybir.ActivationFunctionType
ALU = mybir.AluOpType
AX = mybir.AxisListType
bfdt = ml_dtypes.bfloat16

B, T, D = 16, 2048, 2048
H, HKV, HD = 32, 8, 64
NSEQ = 2
CH = 512
NT = CH // 128
NCH = T // CH
NDC = D // 128
EPS = 1e-5

ENG = ("tensor", "vector", "scalar", "gpsimd", "sync")
ENGSEM = {"tensor": "pe", "vector": "dve", "scalar": "act", "gpsimd": "pool"}
QSEM = {"sync": "qs", "gpsimd": "qg", "scalar": "qa"}
SEMS = ("pe", "dve", "act", "pool", "qs", "qg", "qa")


class Sched:

    def __init__(self, nc):
        self.nc = nc
        self.thunks = {e: [] for e in ENG}
        self.count = {s: 0 for s in SEMS}
        self.hw = {e: {s: 0 for s in SEMS} for e in ENG}

    def _wait(self, eng, sem, val):
        if val <= self.hw[eng][sem]:
            return
        self.hw[eng][sem] = val
        self.thunks[eng].append(("wait", sem, val))

    def _deps(self, eng, deps):
        for d in deps:
            if d is None:
                continue
            if isinstance(d, list):
                self._deps(eng, d)
                continue
            sem, val = d
            self._wait(eng, sem, val)

    def op(self, eng, fn, deps=()):
        self._deps(eng, deps)
        self.thunks[eng].append(("op", fn))

    def sig(self, eng):
        sem = ENGSEM[eng]
        self.count[sem] += 1
        self.thunks[eng].append(("sig", sem))
        return (sem, self.count[sem])

    def dma(self, eng, fn, deps=()):
        qsem = QSEM[eng]
        self._deps(eng, deps)
        self.count[qsem] += 16
        self.thunks[eng].append(("dma", fn, qsem))
        return (qsem, self.count[qsem])

    def emit(self, block, sems):
        secmap = {"sync": block.sync, "tensor": block.tensor, "vector": block.vector,
                  "scalar": block.scalar, "gpsimd": block.gpsimd}
        for e in ENG:
            items = self.thunks[e]
            if not items:
                continue

            def make(items=items):
                def body(eng):
                    last = None
                    for it in items:
                        if it[0] == "wait":
                            eng.wait_ge(sems[it[1]], it[2])
                        elif it[0] == "op":
                            last = it[1](eng)
                        elif it[0] == "sig":
                            last.then_inc(sems[it[1]], 1)
                        else:
                            it[1](eng).then_inc(sems[it[2]], 16)
                return body
            secmap[e](make())


def vslice(base_ap, part0, nparts, off, pat):
    pstep = base_ap.ap[0][0]
    return bass.AP(base_ap.tensor, base_ap.offset + part0 * pstep + off,
                   [[pstep, nparts]] + [list(p) for p in pat])


class WStream:

    def __init__(self, s, wbuf):
        self.s = s
        self.wbuf = wbuf
        self.user_ev = [None, None]
        self.next_slot = 0
        self.q = []

    def prefetch(self, loader, extra_deps=()):
        slot = self.next_slot
        deps = [d for d in extra_deps if d is not None]
        if self.user_ev[slot] is not None:
            deps.append(self.user_ev[slot])
        ev = loader(slot, deps)
        self.q.append((slot, ev))
        self.next_slot = 1 - slot
        return slot, ev

    def current(self):
        return self.q[0]

    def consumed(self, pe_ev):
        slot, _ = self.q.pop(0)
        self.user_ev[slot] = pe_ev


def build_program(nseq=NSEQ, nch=NCH, dbg=False, dbgc=0):
    nc = bass.Bass("TRN2", target_bir_lowering=False)
    dbg_outs = {}

    x_d = nc.dram_tensor("x", [NSEQ, T, D], F32, kind="ExternalInput").ap()
    out_d = nc.dram_tensor("out", [NSEQ, T, D], F32, kind="ExternalOutput").ap()
    w_in = {}
    for name, shp in (("wq", [D, D]), ("wk", [512, D]), ("wv", [512, D]), ("wo", [D, D]),
                      ("up", [D, D]), ("down", [D, D]), ("expert", [D, D]), ("gate", [8, D])):
        w_in[name] = nc.dram_tensor(name, shp, F32, kind="ExternalInput").ap()
    cosq_d = nc.dram_tensor("cosq", [T, 32], F32, kind="ExternalInput").ap()
    sinq_d = nc.dram_tensor("sinq", [T, 32], F32, kind="ExternalInput").ap()
    cosk_d = nc.dram_tensor("cosk", [T, 32], F32, kind="ExternalInput").ap()
    sink_d = nc.dram_tensor("sink", [T, 32], F32, kind="ExternalInput").ap()
    ident_d = nc.dram_tensor("ident", [128, 128], BF16, kind="ExternalInput").ap()
    mask_d = nc.dram_tensor("masktile", [128, 128], BF16, kind="ExternalInput").ap()

    scr = {}
    for name, n_ot in (("wq", 16), ("wo", 16), ("up", 16), ("down", 16), ("wk", 4), ("wv", 4)):
        scr[name] = nc.dram_tensor(name + "T_s", [NDC, 128, n_ot, 128], BF16, kind="Internal").ap()

    with ExitStack() as st:
        def sb(name, shape, dt):
            return st.enter_context(nc.sbuf_tensor(name, shape, dt))

        def psum(name, shape, dt):
            return st.enter_context(nc.psum_tensor(name, shape, dt))

        x_sb = [sb(f"x{i}", [128, D], F32) for i in range(NT)]
        lnT = [sb(f"lnT{i}", [128, D], BF16) for i in range(NT)]
        kT = [sb(f"kT{j}", [128, T], BF16) for j in range(4)]
        v_sb = [sb(f"v{g}", [128, 8 * 65], BF16) for g in range(16)]
        cq_sb = sb("cq", [128, 16 * 32], F32)
        sq_sb = sb("sq", [128, 16 * 32], F32)
        ck_sb = sb("ck", [128, 16 * 32], F32)
        sk_sb = sb("sk", [128, 16 * 32], F32)
        id_bf = sb("idbf", [128, 128], BF16)
        mk_bf = sb("mkbf", [128, 128], BF16)
        gT_sb = sb("gT", [128, NDC * 8], BF16)
        ones_bf = sb("ones", [128, 64], BF16)
        wbuf = [sb(f"wbuf{k}", [128, NDC * 512], BF16) for k in range(2)]
        ar1_bf = sb("ar1bf", [128, 8192], BF16)
        ar2_bf = sb("ar2bf", [128, 8192], BF16)
        ar3_bf = sb("ar3bf", [128, 4096], BF16)
        ar1_f = sb("ar1f", [128, 8192], F32)
        ar2_f = sb("ar2f", [128, 2048], F32)
        rl_bf = sb("rlbf", [128, 1024], BF16)
        stat = sb("stat", [128, 64], F32)
        eps_sb = sb("eps", [128, 1], F32)
        g_sb = sb("gsb", [128, NT], F32)
        dbgv = sb("dbgv", [128, 16], F32)
        dbgv2 = sb("dbgv2", [128, 8], F32)

        psS = psum("psS", [128, 2048], F32)
        psV = psum("psV", [128, 1024], F32)
        psT = psum("psT", [128, 1024], BF16)
        psX = psum("psX", [128, 512], F32)

        sems = {m: st.enter_context(nc.semaphore(m)) for m in SEMS}
        s = Sched(nc)

        pb = [psS[:, 0:512], psS[:, 512:1024], psS[:, 1024:1536], psS[:, 1536:2048]]
        scores = [psS[:, 0:1024], psS[:, 1024:2048]]

        pb_reader = [None] * 4
        psV_reader = None
        psX_reader = None
        psT_reader = None
        pe_tail = None
        act_ar3 = None

        ev_consts = []
        for dst, src in ((cq_sb, cosq_d), (sq_sb, sinq_d), (ck_sb, cosk_d), (sk_sb, sink_d)):
            ev_consts.append(s.dma("sync", lambda e, dst=dst, src=src: e.dma_start(
                out=dst[:].rearrange("p (g c) -> p g c", c=32),
                in_=src.rearrange("(g p) c -> p g c", p=128))))
        ev_consts.append(s.dma("sync", lambda e: e.dma_start(out=id_bf[:], in_=ident_d)))
        ev_consts.append(s.dma("sync", lambda e: e.dma_start(out=mk_bf[:], in_=mask_d)))
        s.op("vector", lambda e: e.memset(ones_bf[:], 1.0))
        ev_ones = s.sig("vector")
        s.op("vector", lambda e: e.memset(eps_sb[:], EPS))
        ev_eps = s.sig("vector")
        for g in range(16):
            s.op("gpsimd", lambda e, g=g: e.memset(
                vslice(v_sb[g][:], 0, 128, 64, [[65, 8], [1, 1]]), 1.0))
        ev_vones = s.sig("gpsimd")

        ev_wprep = []
        prep_done = {}

        prep_state = {"tp": [None, None], "out": [None, None], "ot": 0}

        def prep_T(name):
            W, WT = w_in[name], scr[name]
            n_ot = WT.tensor.shape[2]
            nonlocal psT_reader, pe_tail
            ev_tp = prep_state["tp"]
            ev_out = prep_state["out"]
            for ot_l in range(n_ot):
                ot = ot_l
                sl = (prep_state["ot"] + ot_l) % 2
                bft = ar3_bf[:, sl * 2048:sl * 2048 + 2048]
                d_in = s.dma("gpsimd", lambda e, W=W, ot=ot, bft=bft: e.dma_start(
                    out=bft, in_=W[ot * 128:(ot + 1) * 128, :]),
                    deps=[ev_tp[sl]])
                ev_ev = None
                stg = ar2_bf[:, 4096 + sl * 2048:4096 + sl * 2048 + 2048]
                for half in range(2):
                    for k in range(8):
                        dcc = half * 8 + k
                        deps = [d_in, ev_consts[4]]
                        if psT_reader:
                            deps.append(psT_reader)
                        s.op("tensor", lambda e, dcc=dcc, k=k, bft=bft: e.transpose(
                            out=psT[:, k * 128:(k + 1) * 128],
                            in_=vslice(bft, 0, 128, dcc * 128, [[1, 128]]),
                            identity=id_bf[:]), deps=deps if k == 0 else [])
                    ev_t = s.sig("tensor")
                    pe_tail = ev_t
                    dep_e = [ev_t, ev_out[sl]]
                    s.op("vector", lambda e, half=half, stg=stg: e.tensor_copy(
                        out=vslice(stg, 0, 128, half * 1024, [[1, 1024]]), in_=psT[:]),
                        deps=dep_e)
                    psT_reader = s.sig("vector")
                ev_tp[sl] = pe_tail
                ev_out[sl] = s.dma("scalar", lambda e, WT=WT, ot=ot, stg=stg: e.dma_start(
                    out=WT[:, :, ot, :].rearrange("dc p o -> p dc o"),
                    in_=stg.rearrange("p (dc o) -> p dc o", o=128)),
                    deps=[psT_reader])
            prep_state["ot"] += n_ot
            prep_done[name] = [ev_out[0], ev_out[1]]
            ev_wprep.append(ev_out[0])
            ev_wprep.append(ev_out[1])

        for nm in ("wk", "wv"):
            prep_T(nm)

        d_in = s.dma("gpsimd", lambda e: e.dma_start(out=ar3_bf[0:8, 0:2048], in_=w_in["gate"]),
                     deps=[pe_tail])
        for dc in range(NDC):
            deps = [d_in] + ([psT_reader] if psT_reader else [])
            s.op("tensor", lambda e, dc=dc: e.transpose(
                out=psT[:, dc * 8:(dc + 1) * 8],
                in_=vslice(ar3_bf[0:8, 0:2048], 0, 8, dc * 128, [[1, 128]]),
                identity=id_bf[0:8, 0:8]), deps=deps if dc == 0 else [])
        ev_t = s.sig("tensor")
        pe_tail = ev_t
        s.op("vector", lambda e: e.tensor_copy(out=gT_sb[:], in_=psT[:, 0:128]), deps=[ev_t])
        psT_reader = s.sig("vector")
        ev_gT = psT_reader
        prep_state["tp"][0] = pe_tail

        kin = [vslice(kT[0][:], 0, 128, 1024, [[1, 1024]]),
               vslice(kT[1][:], 0, 128, 1024, [[1, 1024]])]
        kst = [vslice(kT[2][:], 0, 128, 1024, [[1, 1024]]),
               vslice(kT[3][:], 0, 128, 1024, [[1, 1024]])]
        prep2 = {"tp": None, "out": None}

        def prep_T2(name):
            W, WT = w_in[name], scr[name]
            n_ot = WT.tensor.shape[2]
            nonlocal psT_reader, pe_tail
            evs_out = []
            for ot in range(n_ot):
                d_ins = []
                for hf in range(2):
                    d_ins.append(s.dma("gpsimd", lambda e, W=W, ot=ot, hf=hf: e.dma_start(
                        out=kin[hf], in_=W[ot * 128:(ot + 1) * 128, hf * 1024:(hf + 1) * 1024]),
                        deps=[prep2["tp"]]))
                ev_evs = []
                for hf in range(2):
                    for k in range(8):
                        dcc = hf * 8 + k
                        deps = d_ins + [ev_consts[4]] + ([psT_reader] if psT_reader else [])
                        s.op("tensor", lambda e, dcc=dcc, k=k, hf=hf: e.transpose(
                            out=psT[:, k * 128:(k + 1) * 128],
                            in_=vslice(kin[hf], 0, 128, (dcc % 8) * 128, [[1, 128]]),
                            identity=id_bf[:]), deps=deps if k == 0 else [])
                    ev_t = s.sig("tensor")
                    pe_tail = ev_t
                    s.op("vector", lambda e, hf=hf: e.tensor_copy(
                        out=kst[hf], in_=psT[:]), deps=[ev_t, prep2["out"]])
                    psT_reader = s.sig("vector")
                    ev_evs.append(psT_reader)
                prep2["tp"] = pe_tail
                d_outs = []
                for hf in range(2):
                    d_outs.append(s.dma("scalar", lambda e, WT=WT, ot=ot, hf=hf: e.dma_start(
                        out=WT[8 * hf:8 * hf + 8, :, ot, :].rearrange("dc p o -> p dc o"),
                        in_=kst[hf].rearrange("p (dc o) -> p dc o", o=128)),
                        deps=[ev_evs[hf]]))
                prep2["out"] = d_outs
                evs_out = d_outs
            prep_done[name] = evs_out
            ev_wprep.extend(evs_out)

        for nm in ("wq", "wo", "up", "down"):
            prep_T2(nm)


        def ln_stats_only(tile_ap, dep_evs, tl):
            for q in range(4):
                s.op("vector", lambda e, q=q: e.bn_stats(
                    out=stat[:, 32 + 6 * q:38 + 6 * q],
                    in_=vslice(tile_ap, 0, 128, q * 512, [[1, 512]])),
                    deps=dep_evs if q == 0 else [])
            ev_bn = s.sig("vector")
            s.op("vector", lambda e, tl=tl: e.bn_aggr(
                out=stat[:, 8 + 2 * tl:10 + 2 * tl], in_=stat[:, 32:56]), deps=[ev_bn])
            return s.sig("vector")

        def ln_fin(tl, dep):
            s.op("scalar", lambda e, tl=tl: e.activation(
                out=stat[:, 56 + tl:57 + tl], in_=stat[:, 9 + 2 * tl:10 + 2 * tl],
                func=AF.Ln, bias=eps_sb[:]), deps=[dep, ev_eps])
            ev_lnv = s.sig("scalar")
            s.op("scalar", lambda e, tl=tl: e.activation(
                out=stat[:, 2 * tl:2 * tl + 1], in_=stat[:, 56 + tl:57 + tl],
                func=AF.Exp, scale=-0.5), deps=[ev_lnv])
            ev_rstd = s.sig("scalar")
            s.op("vector", lambda e, tl=tl: e.tensor_scalar(
                out=stat[:, 2 * tl + 1:2 * tl + 2], in0=stat[:, 8 + 2 * tl:9 + 2 * tl],
                scalar1=stat[:, 2 * tl:2 * tl + 1], scalar2=-1.0,
                op0=ALU.mult, op1=ALU.mult), deps=[ev_rstd])
            return s.sig("vector")

        def ln_stats(tile_ap, dep_evs, tl=0):
            return ln_fin(tl, ln_stats_only(tile_ap, dep_evs, tl))

        def transpose16(src_bf, dst_ap, dep_evs):
            nonlocal psT_reader, pe_tail
            pe_ev = None
            for half in range(2):
                for k in range(8):
                    dcc = half * 8 + k
                    deps = list(dep_evs) + ([psT_reader] if psT_reader else [])
                    s.op("tensor", lambda e, dcc=dcc, k=k: e.transpose(
                        out=psT[:, k * 128:(k + 1) * 128],
                        in_=vslice(src_bf, 0, 128, dcc * 128, [[1, 128]]),
                        identity=id_bf[:]), deps=deps if k == 0 else [])
                pe_ev = s.sig("tensor")
                pe_tail = pe_ev
                s.op("vector", lambda e, half=half: e.tensor_copy(
                    out=vslice(dst_ap, 0, 128, half * 1024, [[1, 1024]]),
                    in_=psT[:]), deps=[pe_ev])
                psT_reader = s.sig("vector")
            return psT_reader

        def rope6(ps_src, nblk, cos_t, sin_t, gtile, out_bf, dep_evs):
            cs = vslice(cos_t[:], 0, 128, gtile * 32, [[0, nblk], [1, 32]])
            sn = vslice(sin_t[:], 0, 128, gtile * 32, [[0, nblk], [1, 32]])
            a_v = vslice(ps_src, 0, 128, 0, [[64, nblk], [1, 32]])
            b_v = vslice(ps_src, 0, 128, 32, [[64, nblk], [1, 32]])
            t1 = vslice(ar2_f[:], 0, 128, 0, [[32, nblk], [1, 32]])
            t2 = vslice(ar2_f[:], 0, 128, 512, [[32, nblk], [1, 32]])
            oa = vslice(out_bf, 0, 128, 0, [[64, nblk], [1, 32]])
            ob = vslice(out_bf, 0, 128, 32, [[64, nblk], [1, 32]])
            s.op("vector", lambda e: e.tensor_tensor(out=t1, in0=a_v, in1=cs, op=ALU.mult),
                 deps=dep_evs)
            s.op("vector", lambda e: e.tensor_tensor(out=t2, in0=b_v, in1=sn, op=ALU.mult))
            ev_r1 = s.sig("vector")
            s.op("vector", lambda e: e.tensor_tensor(out=oa, in0=t1, in1=t2, op=ALU.subtract),
                 deps=[ev_r1])
            s.op("vector", lambda e: e.tensor_tensor(out=t1, in0=a_v, in1=sn, op=ALU.mult))
            s.op("vector", lambda e: e.tensor_tensor(out=t2, in0=b_v, in1=cs, op=ALU.mult))
            ev_r2 = s.sig("vector")
            s.op("vector", lambda e: e.tensor_tensor(out=ob, in0=t1, in1=t2, op=ALU.add),
                 deps=[ev_r2])
            return s.sig("vector")

        def mm_proj(bank_i, lhsT_src, lhs_off, rhs_buf, dc, start, stop):
            return lambda e: e.matmul(
                out=pb[bank_i],
                lhsT=vslice(lhsT_src, 0, 128, lhs_off + dc * 128, [[1, 128]]),
                rhs=vslice(rhs_buf[:], 0, 128, dc * 512, [[1, 512]]),
                start=start, stop=stop)

        ev_out_dma = {}
        consts_dep = ev_consts + [ev_ones, ev_vones, ev_gT, ev_eps]
        ws = WStream(s, wbuf)

        def mk_loader(scr_ap, oc, pname=None):
            def loader(slot, deps):
                dd = list(deps)
                if pname is not None:
                    dd.append(prep_done[pname])
                return s.dma("sync", lambda e: e.dma_start(
                    out=wbuf[slot][:].rearrange("p (dc o) -> p dc o", dc=NDC),
                    in_=scr_ap[:, :, oc * 4:(oc + 1) * 4, :].rearrange("dc p ot o -> p dc (ot o)")),
                    deps=dd)
            return loader

        def mk_loader_small(scr_ap):
            def loader(slot, deps):
                return s.dma("sync", lambda e: e.dma_start(
                    out=wbuf[slot][:].rearrange("p (dc o) -> p dc o", dc=NDC),
                    in_=scr_ap[:, :, :, :].rearrange("dc p ot o -> p dc (ot o)")),
                    deps=deps)
            return loader

        def mk_loader_flat(oc):
            def loader(slot, deps):
                return s.dma("gpsimd", lambda e: e.dma_start(
                    out=wbuf[slot][:].rearrange("p (dc o) -> p dc o", dc=NDC),
                    in_=w_in["expert"].rearrange("(dc p) o -> p dc o", p=128)[:, :, oc * 512:(oc + 1) * 512]),
                    deps=deps)
            return loader

        def tap(name, src_ap, shape, dt, ev):
            if not dbg:
                return None
            dout = nc.dram_tensor("dbg_" + name, shape, dt, kind="ExternalOutput").ap()
            dbg_outs[name] = dout
            return s.dma("sync", lambda e: e.dma_start(out=dout, in_=src_ap), deps=[ev])

        xA = [x_sb[i][:] for i in range(NT)]
        xB = [vslice(ar1_f[:], 0, 128, i * 2048, [[1, 2048]]) for i in range(NT)]
        ev_staging_last = [None] * NT
        ln_slot_war = [None, None]
        ev_x_out_last = [None] * NT

        ln_slot_war = list(prep_state["tp"])
        prep_tail = [v for v in prep_done.get("down", []) if v is not None]

        first = True
        cg = -1
        for seq in range(nseq):
            for c in range(nch):
                cg += 1
                xt = xA if cg % 2 == 0 else xB
                stg = xB if cg % 2 == 0 else xA
                dep0 = consts_dep if first else []
                ws.prefetch(mk_loader_small(scr["wk"]), dep0 + ([prep_done["wk"]] if first else []))
                ws.prefetch(mk_loader_small(scr["wv"]), dep0 + ([prep_done["wv"]] if first else []))

                ev_lnT = [None] * NT
                ev_kv_pe = None
                ev_v_evac = None
                for i in range(NT):
                    g = c * NT + i
                    t0 = g * 128
                    depx = dep0 + [ev_staging_last[i]]
                    d_x = s.dma("sync", lambda e, i=i, t0=t0, seq=seq, xt=xt: e.dma_start(
                        out=xt[i], in_=x_d[seq, t0:t0 + 128, :]), deps=depx)
                    ev_st = ln_stats(xt[i], [d_x], i)
                    sl1 = i % 2
                    ln1 = ar3_bf[:, sl1 * 2048:sl1 * 2048 + 2048]
                    dep_ln = [ev_st, ln_slot_war[sl1], act_ar3]
                    s.op("scalar", lambda e, i=i, ln1=ln1, xt=xt: e.activation(
                        out=ln1, in_=xt[i], func=AF.Identity,
                        scale=stat[:, 2 * i:2 * i + 1], bias=stat[:, 2 * i + 1:2 * i + 2]),
                        deps=dep_ln)
                    ev_ln1 = s.sig("scalar")
                    if dbg and seq == 0 and c == dbgc and i == 0:
                        s.op("vector", lambda e: e.tensor_copy(out=dbgv2[:], in_=stat[:, 0:8]))
                        ev_d2 = s.sig("vector")
                        tap("lnstat0", dbgv2[:], [128, 8], F32, ev_d2)
                    ev_lnT[i] = transpose16(ln1, lnT[i][:], [ev_ln1])
                    ln_slot_war[sl1] = pe_tail
                    (slk, d_wk) = ws.q[0]
                    (slv, d_wv) = ws.q[1]
                    for dc in range(NDC):
                        deps = [ev_lnT[i], d_wk, d_wv, psV_reader] if dc == 0 else []
                        s.op("tensor", lambda e, i=i, dc=dc, slk=slk: e.matmul(
                            out=psV[:, 0:512],
                            lhsT=vslice(lnT[i][:], 0, 128, dc * 128, [[1, 128]]),
                            rhs=vslice(wbuf[slk][:], 0, 128, dc * 512, [[1, 512]]),
                            start=(dc == 0), stop=(dc == NDC - 1)), deps=deps)
                        s.op("tensor", lambda e, i=i, dc=dc, slv=slv: e.matmul(
                            out=psV[:, 512:1024],
                            lhsT=vslice(lnT[i][:], 0, 128, dc * 128, [[1, 128]]),
                            rhs=vslice(wbuf[slv][:], 0, 128, dc * 512, [[1, 512]]),
                            start=(dc == 0), stop=(dc == NDC - 1)))
                    ev_kvmm = s.sig("tensor")
                    pe_tail = ev_kvmm
                    ev_kv_pe = ev_kvmm
                    krope = ar2_bf[:, (i % 2) * 512:(i % 2) * 512 + 512]
                    ev_krope = rope6(psV[:, 0:512], 8, ck_sb, sk_sb, g, krope, [ev_kvmm])
                    for j in range(4):
                        deps = [ev_krope] + ([psT_reader] if psT_reader else [])
                        s.op("tensor", lambda e, j=j, krope=krope: e.transpose(
                            out=psT[:, j * 128:(j + 1) * 128],
                            in_=vslice(krope, 0, 128, j * 128, [[1, 128]]),
                            identity=id_bf[:]), deps=deps if j == 0 else [])
                    ev_kt = s.sig("tensor")
                    pe_tail = ev_kt
                    for j in range(4):
                        depk = [ev_kt] if j == 0 else []
                        if cg == 2 and i == 0 and j == 0:
                            depk = depk + prep_tail
                        s.op("vector", lambda e, j=j, g=g: e.tensor_copy(
                            out=kT[j][:, g * 128:(g + 1) * 128],
                            in_=psT[:, j * 128:(j + 1) * 128]), deps=depk)
                    psT_reader = s.sig("vector")
                    s.op("scalar", lambda e, g=g: e.activation(
                        out=vslice(v_sb[g][:], 0, 128, 0, [[65, 8], [1, 64]]),
                        in_=vslice(psV[:], 0, 128, 512, [[64, 8], [1, 64]]),
                        func=AF.Copy), deps=[ev_kvmm, ev_vones])
                    ev_v_evac = s.sig("scalar")
                    psV_reader = ev_v_evac
                ev_kT_ready = psT_reader
                ws.consumed(ev_kv_pe)
                ws.consumed(ev_kv_pe)
                if dbg and seq == 0 and c == dbgc:
                    tap("lnT0", lnT[0][:], [128, 2048], BF16, ev_lnT[0])
                    tap("kT0", kT[0][:, 0:512], [128, 512], BF16, ev_kT_ready)
                    tap("v0", v_sb[0][:], [128, 520], BF16, ev_v_evac)

                ws.prefetch(mk_loader(scr["wq"], 0, "wq"))
                qT_base = ar1_bf[:]
                ev_qT = None
                ev_qt_prev = None
                for oc in range(4):
                    (slot, d_w) = ws.current()
                    ev_mm = [None] * NT
                    for tl in range(NT):
                        deps = []
                        if tl == 0:
                            deps = [d_w] + [ev_lnT[t2_] for t2_ in range(NT)]
                        deps.append(pb_reader[tl])
                        for dc in range(NDC):
                            s.op("tensor", mm_proj(tl, lnT[tl][:], 0, wbuf[slot], dc,
                                                   dc == 0, dc == NDC - 1),
                                 deps=deps if dc == 0 else [])
                        ev_mm[tl] = s.sig("tensor")
                        pe_tail = ev_mm[tl]
                    ws.consumed(ev_mm[3])
                    if oc < 3:
                        ws.prefetch(mk_loader(scr["wq"], oc + 1, "wq"))
                    else:
                        ws.prefetch(mk_loader(scr["wo"], 0, "wo"))
                    for tl in range(NT):
                        g = c * NT + tl
                        qrope = ar2_bf[:, 1024 + (tl % 2) * 512:1024 + (tl % 2) * 512 + 512]
                        ev_qr = rope6(pb[tl], 8, cq_sb, sq_sb, g, qrope,
                                      [ev_mm[tl], ev_qt_prev])
                        pb_reader[tl] = ev_qr
                        for ci in range(4):
                            deps = [ev_qr] + ([psT_reader] if psT_reader else [])
                            s.op("tensor", lambda e, ci=ci, qrope=qrope: e.transpose(
                                out=psT[:, ci * 128:(ci + 1) * 128],
                                in_=vslice(qrope, 0, 128, ci * 128, [[1, 128]]),
                                identity=id_bf[:]), deps=deps if ci == 0 else [])
                        ev_qt = s.sig("tensor")
                        pe_tail = ev_qt
                        ev_qt_prev = ev_qt
                        for ci in range(4):
                            cp = oc * 4 + ci
                            s.op("vector", lambda e, cp=cp, tl=tl, ci=ci: e.tensor_copy(
                                out=vslice(qT_base, 0, 128, cp * 512 + tl * 128, [[1, 128]]),
                                in_=psT[:, ci * 128:(ci + 1) * 128]),
                                deps=[ev_qt] if ci == 0 else [])
                        psT_reader = s.sig("vector")
                ev_qT = psT_reader
                if dbg and seq == 0 and c == dbgc:
                    tap("qT", ar1_bf[:], [128, 8192], BF16, ev_qT)

                nkt = (c + 1) * NT
                ev_exp_slot = [None, None]
                ev_av_slot = [None, None]
                ev_yT = []
                pending_epi = None
                ev_rl_pe = None
                av_evac_ev = [None, None]
                for p in range(16):
                    j = p // 4
                    kvA, kvB = 2 * j, 2 * j + 1
                    ev_avlast = None
                    pending_av = None
                    for ktc in range(nkt):
                        sl = ktc % 2
                        diag = ktc >= c * NT
                        qt_off = 128 * (ktc - c * NT) if diag else 0
                        N = 512 - qt_off
                        deps = [ev_qT, ev_kT_ready, ev_exp_slot[sl]]
                        s.op("tensor", lambda e, j=j, p=p, ktc=ktc, sl=sl, qt_off=qt_off, N=N: e.matmul(
                            out=psS[:, sl * 1024 + qt_off:sl * 1024 + qt_off + N],
                            lhsT=kT[j][0:64, ktc * 128:(ktc + 1) * 128],
                            rhs=vslice(qT_base, 0, 64, p * 512 + qt_off, [[1, N]]),
                            start=True, stop=True), deps=deps)
                        s.op("tensor", lambda e, j=j, p=p, ktc=ktc, sl=sl, qt_off=qt_off, N=N: e.matmul(
                            out=psS[:, sl * 1024 + 512 + qt_off:sl * 1024 + 512 + qt_off + N],
                            lhsT=kT[j][64:128, ktc * 128:(ktc + 1) * 128],
                            rhs=vslice(qT_base, 64, 64, p * 512 + qt_off, [[1, N]]),
                            start=True, stop=True))
                        ev_sc = s.sig("tensor")
                        pe_tail = ev_sc
                        if pending_av is not None:
                            pending_av()
                            pending_av = None
                        deps_e = [ev_sc, ev_av_slot[sl]]
                        s.op("scalar", lambda e, sl=sl, qt_off=qt_off, N=N: e.activation(
                            out=vslice(ar3_bf[:], 0, 128, sl * 1024 + qt_off, [[512, 2], [1, N]]),
                            in_=vslice(psS[:], 0, 128, sl * 1024 + qt_off, [[512, 2], [1, N]]),
                            func=AF.Exp), deps=deps_e)
                        ev_exp = s.sig("scalar")
                        act_ar3 = ev_exp

                        def make_av(ktc=ktc, sl=sl, diag=diag, qt_off=qt_off, N=N,
                                    ev_exp=ev_exp, kvA=kvA, kvB=kvB, nkt=nkt):
                            def do_av():
                                nonlocal ev_avlast, pe_tail
                                ev_mask = None
                                if diag:
                                    for half in range(2):
                                        s.op("vector", lambda e, sl=sl, qt_off=qt_off, half=half: e.tensor_tensor(
                                            out=vslice(ar3_bf[:], 0, 128, sl * 1024 + half * 512 + qt_off, [[1, 128]]),
                                            in0=vslice(ar3_bf[:], 0, 128, sl * 1024 + half * 512 + qt_off, [[1, 128]]),
                                            in1=mk_bf[:], op=ALU.mult),
                                            deps=[ev_exp, ev_consts[5]] if half == 0 else [])
                                    ev_mask = s.sig("vector")
                                deps_av = [ev_exp, ev_mask]
                                if ktc == 0:
                                    deps_av += [psV_reader, ev_vones]
                                s.op("tensor", lambda e, kvA=kvA, ktc=ktc, sl=sl, qt_off=qt_off, N=N, nkt=nkt: e.matmul(
                                    out=psV[0:65, qt_off:qt_off + N],
                                    lhsT=v_sb[ktc][:, kvA * 65:(kvA + 1) * 65],
                                    rhs=vslice(ar3_bf[:], 0, 128, sl * 1024 + qt_off, [[1, N]]),
                                    start=(ktc == 0), stop=(ktc == nkt - 1)), deps=deps_av)
                                s.op("tensor", lambda e, kvB=kvB, ktc=ktc, sl=sl, qt_off=qt_off, N=N, nkt=nkt: e.matmul(
                                    out=psV[0:65, 512 + qt_off:512 + qt_off + N],
                                    lhsT=v_sb[ktc][:, kvB * 65:(kvB + 1) * 65],
                                    rhs=vslice(ar3_bf[:], 0, 128, sl * 1024 + 512 + qt_off, [[1, N]]),
                                    start=(ktc == 0), stop=(ktc == nkt - 1)))
                                ev_av = s.sig("tensor")
                                pe_tail = ev_av
                                ev_av_slot[sl] = ev_av
                                ev_avlast = ev_av
                            return do_av
                        ev_exp_slot[sl] = ev_exp
                        pending_av = make_av()
                        if ktc == 1 and pending_epi is not None:
                            pending_epi()
                            pending_epi = None
                    if pending_av is not None:
                        pending_av()
                        pending_av = None
                    aslot = p % 2
                    dep_c = [ev_avlast, av_evac_ev[aslot]]
                    s.op("vector", lambda e, aslot=aslot: e.tensor_copy(
                        out=vslice(ar2_f[:], 0, 65, aslot * 1024, [[1, 1024]]),
                        in_=psV[0:65, 0:1024]), deps=dep_c)
                    ev_avc = s.sig("vector")
                    psV_reader = ev_avc

                    def make_epi(p=p, aslot=aslot, ev_avc=ev_avc):
                        def epi():
                            nonlocal ev_rl_pe, psX_reader, pe_tail
                            s.op("scalar", lambda e: e.activation(
                                out=vslice(ar2_f[:], 64, 1, aslot * 1024, [[1, 1024]]),
                                in_=vslice(ar2_f[:], 64, 1, aslot * 1024, [[1, 1024]]),
                                func=AF.Ln), deps=[ev_avc])
                            ev_lnl = s.sig("scalar")
                            dep_cast = [ev_lnl, ev_rl_pe]
                            s.op("scalar", lambda e: e.activation(
                                out=rl_bf[64:65, 0:1024],
                                in_=vslice(ar2_f[:], 64, 1, aslot * 1024, [[1, 1024]]),
                                func=AF.Exp, scale=-1.0), deps=dep_cast)
                            ev_rlb = s.sig("scalar")
                            dep_bc = [ev_rlb, ev_ones, psX_reader]
                            s.op("tensor", lambda e: e.matmul(
                                out=psX[0:64, :], lhsT=ones_bf[64:65, :], rhs=rl_bf[64:65, 0:512],
                                start=True, stop=True), deps=dep_bc)
                            ev_bcA = s.sig("tensor")
                            s.op("vector", lambda e: e.tensor_copy(
                                out=vslice(ar3_bf[:], 0, 64, 2048, [[1, 512]]),
                                in_=psX[0:64, :]), deps=[ev_bcA])
                            ev_evA = s.sig("vector")
                            s.op("tensor", lambda e: e.matmul(
                                out=psX[0:64, :], lhsT=ones_bf[64:65, :], rhs=rl_bf[64:65, 512:1024],
                                start=True, stop=True), deps=[ev_evA])
                            ev_bcB = s.sig("tensor")
                            pe_tail = ev_bcB
                            ev_rl_pe = ev_bcB
                            s.op("vector", lambda e: e.tensor_copy(
                                out=vslice(ar3_bf[:], 0, 64, 2560, [[1, 512]]),
                                in_=psX[0:64, :]), deps=[ev_bcB])
                            ev_rbc = s.sig("vector")
                            psX_reader = ev_rbc
                            s.op("vector", lambda e, p=p: e.tensor_tensor(
                                out=vslice(ar2_bf[:], 0, 64, p * 512, [[1, 512]]),
                                in0=vslice(ar2_f[:], 0, 64, aslot * 1024, [[1, 512]]),
                                in1=vslice(ar3_bf[:], 0, 64, 2048, [[1, 512]]),
                                op=ALU.mult), deps=[ev_rbc] + (prep_tail if cg == 0 else []))
                            s.op("vector", lambda e, p=p: e.tensor_tensor(
                                out=vslice(ar2_bf[:], 64, 64, p * 512, [[1, 512]]),
                                in0=vslice(ar2_f[:], 0, 64, aslot * 1024 + 512, [[1, 512]]),
                                in1=vslice(ar3_bf[:], 0, 64, 2560, [[1, 512]]),
                                op=ALU.mult))
                            ev_y = s.sig("vector")
                            ev_yT.append(ev_y)
                            av_evac_ev[aslot] = ev_y
                        return epi
                    pending_epi = make_epi()
                if pending_epi is not None:
                    pending_epi()
                    pending_epi = None

                attn_rows = stg
                ev_evac = [[None] * 4 for _ in range(NT)]
                for oc in range(4):
                    (slot, d_w) = ws.current()
                    ev_mm = [None] * NT
                    for tl in range(NT):
                        deps = [pb_reader[tl]]
                        if tl == 0:
                            deps += [d_w, ev_yT[-1]]
                        for dyc in range(NDC):
                            s.op("tensor", lambda e, tl=tl, dyc=dyc, slot=slot: e.matmul(
                                out=pb[tl],
                                lhsT=vslice(ar2_bf[:], 0, 128, dyc * 512 + tl * 128, [[1, 128]]),
                                rhs=vslice(wbuf[slot][:], 0, 128, dyc * 512, [[1, 512]]),
                                start=(dyc == 0), stop=(dyc == NDC - 1)),
                                deps=deps if dyc == 0 else [])
                        ev_mm[tl] = s.sig("tensor")
                        pe_tail = ev_mm[tl]
                    ws.consumed(ev_mm[3])
                    if oc < 3:
                        ws.prefetch(mk_loader(scr["wo"], oc + 1, "wo"))
                    else:
                        ws.prefetch(mk_loader(scr["up"], 0, "up"))
                    for tl in range(NT):
                        dep_ev = [ev_mm[tl]]
                        if oc == 0:
                            dep_ev.append(ev_x_out_last[tl])
                        s.op("scalar", lambda e, tl=tl, oc=oc, stg=stg: e.activation(
                            out=vslice(stg[tl], 0, 128, oc * 512, [[1, 512]]),
                            in_=pb[tl], func=AF.Copy), deps=dep_ev)
                        ev_evac[tl][oc] = s.sig("scalar")
                        pb_reader[tl] = ev_evac[tl][oc]
                if dbg and seq == 0 and c == dbgc:
                    for _tl in range(NT):
                        tap(f"attn{_tl}", stg[_tl], [128, 2048], F32, ev_evac[_tl][3])
                ev_x2 = [None] * NT
                ev_sto = [None] * NT
                for tl in range(NT):
                    ev_sto[tl] = ln_stats_only(attn_rows[tl], [ev_evac[tl][q] for q in range(4)], tl)
                ev_fin = [ln_fin(tl, ev_sto[tl]) for tl in range(NT)]
                ev_ap = [None] * NT
                for tl in range(NT):
                    s.op("scalar", lambda e, tl=tl, ar=attn_rows: e.activation(
                        out=ar[tl], in_=ar[tl], func=AF.Identity,
                        scale=stat[:, 2 * tl:2 * tl + 1], bias=stat[:, 2 * tl + 1:2 * tl + 2]),
                        deps=[ev_fin[tl]])
                    ev_ap[tl] = s.sig("scalar")
                for tl in range(NT):
                    s.op("gpsimd", lambda e, tl=tl, xt=xt, ar=attn_rows: e.tensor_tensor(
                        out=xt[tl], in0=xt[tl], in1=ar[tl], op=ALU.add), deps=[ev_ap[tl]])
                    ev_x2[tl] = s.sig("gpsimd")
                if dbg and seq == 0 and c == dbgc:
                    tap("x2", xt[0], [128, 2048], F32, ev_x2[0])

                ev_ln2T = [None] * NT
                ev_gate = None
                ev_sto = [ln_stats_only(xt[tl], [ev_x2[tl]], tl) for tl in range(NT)]
                ev_fin = [ln_fin(tl, ev_sto[tl]) for tl in range(NT)]
                for tl in range(NT):
                    sl2 = tl % 2
                    ln2 = ar3_bf[:, sl2 * 2048:sl2 * 2048 + 2048]
                    dep_ln = [ev_fin[tl], ln_slot_war[sl2], act_ar3]
                    s.op("scalar", lambda e, ln2=ln2, tl=tl, xt=xt: e.activation(
                        out=ln2, in_=xt[tl], func=AF.Identity,
                        scale=stat[:, 2 * tl:2 * tl + 1], bias=stat[:, 2 * tl + 1:2 * tl + 2]),
                        deps=dep_ln)
                    ev_ln2 = s.sig("scalar")
                    ev_ln2T[tl] = transpose16(ln2, lnT[tl][:], [ev_ln2])
                    ln_slot_war[sl2] = pe_tail
                    for dc in range(NDC):
                        deps = [ev_ln2T[tl], ev_gT, psX_reader] if dc == 0 else []
                        s.op("tensor", lambda e, tl=tl, dc=dc: e.matmul(
                            out=psX[:, 0:8],
                            lhsT=vslice(lnT[tl][:], 0, 128, dc * 128, [[1, 128]]),
                            rhs=gT_sb[:, dc * 8:(dc + 1) * 8],
                            start=(dc == 0), stop=(dc == NDC - 1)), deps=deps)
                    ev_gmm = s.sig("tensor")
                    pe_tail = ev_gmm
                    s.op("vector", lambda e: e.tensor_reduce(
                        out=stat[:, 24:25], in_=psX[:, 0:8], axis=AX.X, op=ALU.max), deps=[ev_gmm])
                    ev_g1 = s.sig("vector")
                    s.op("vector", lambda e: e.tensor_scalar(
                        out=stat[:, 32:40], in0=psX[:, 0:8], scalar1=stat[:, 24:25], scalar2=None,
                        op0=ALU.is_ge, op1=ALU.bypass), deps=[ev_g1])
                    ev_g2 = s.sig("vector")
                    s.op("vector", lambda e: e.scalar_tensor_tensor(
                        out=stat[:, 40:48], in0=stat[:, 32:40], scalar=-1e30, in1=psX[:, 0:8],
                        op0=ALU.mult, op1=ALU.add), deps=[ev_g2])
                    ev_g3 = s.sig("vector")
                    s.op("vector", lambda e: e.tensor_reduce(
                        out=stat[:, 25:26], in_=stat[:, 40:48], axis=AX.X, op=ALU.max),
                        deps=[ev_g3])
                    ev_g4 = s.sig("vector")
                    s.op("vector", lambda e: e.tensor_tensor(
                        out=stat[:, 26:27], in0=stat[:, 25:26], in1=stat[:, 24:25], op=ALU.subtract),
                        deps=[ev_g4])
                    ev_gd = s.sig("vector")
                    s.op("scalar", lambda e: e.activation(
                        out=stat[:, 27:28], in_=stat[:, 26:27], func=AF.Exp), deps=[ev_gd])
                    ev_gu = s.sig("scalar")
                    s.op("vector", lambda e: e.tensor_scalar(
                        out=stat[:, 28:29], in0=stat[:, 27:28], scalar1=1.0, scalar2=None,
                        op0=ALU.add, op1=ALU.bypass), deps=[ev_gu])
                    ev_g5 = s.sig("vector")
                    s.op("vector", lambda e: e.reciprocal(out=stat[:, 29:30], in_=stat[:, 28:29]),
                         deps=[ev_g5])
                    ev_g6 = s.sig("vector")
                    s.op("vector", lambda e, tl=tl: e.scalar_tensor_tensor(
                        out=g_sb[:, tl:tl + 1], in0=stat[:, 27:28], scalar=stat[:, 29:30],
                        in1=stat[:, 29:30], op0=ALU.mult, op1=ALU.add), deps=[ev_g6])
                    ev_gate = s.sig("vector")
                    psX_reader = ev_gate

                if dbg and seq == 0 and c == dbgc:
                    tap("g", g_sb[:], [128, NT], F32, ev_gate)
                ev_h = [[None] * 4 for _ in range(NT)]
                for oc in range(4):
                    (slot, d_w) = ws.current()
                    ev_mm = [None] * NT
                    for tl in range(NT):
                        deps = [pb_reader[tl], ev_ln2T[tl]]
                        if tl == 0:
                            deps.append(d_w)
                        for dc in range(NDC):
                            s.op("tensor", lambda e, tl=tl, dc=dc, slot=slot: e.matmul(
                                out=pb[tl],
                                lhsT=vslice(lnT[tl][:], 0, 128, dc * 128, [[1, 128]]),
                                rhs=vslice(wbuf[slot][:], 0, 128, dc * 512, [[1, 512]]),
                                start=(dc == 0), stop=(dc == NDC - 1)),
                                deps=deps if dc == 0 else [])
                        ev_mm[tl] = s.sig("tensor")
                        pe_tail = ev_mm[tl]
                    ws.consumed(ev_mm[3])
                    if oc < 3:
                        ws.prefetch(mk_loader(scr["up"], oc + 1, "up"))
                    else:
                        ws.prefetch(mk_loader_flat(0))
                    for tl in range(NT):
                        s.op("scalar", lambda e, tl=tl, oc=oc: e.activation(
                            out=vslice(ar2_bf[:], 0, 128, tl * 2048 + oc * 512, [[1, 512]]),
                            in_=pb[tl], func=AF.Relu),
                            deps=[ev_mm[tl]] + (prep_tail if cg == 0 else []))
                        ev_h[tl][oc] = s.sig("scalar")
                        pb_reader[tl] = ev_h[tl][oc]

                if dbg and seq == 0 and c == dbgc:
                    tap("h", ar2_bf[:], [128, 8192], BF16, ev_h[3][3])
                ev_hT = [None] * NT
                for tl in range(NT):
                    src = vslice(ar2_bf[:], 0, 128, tl * 2048, [[1, 2048]])
                    ev_hT[tl] = transpose16(src, vslice(ar1_bf[:], 0, 128, tl * 2048, [[1, 2048]]),
                                            [ev_h[tl][q] for q in range(4)])

                ffn_rows = stg
                ev_ffn = [[None] * 4 for _ in range(NT)]
                for oc in range(4):
                    (slot_e, d_we) = ws.current()
                    ev_mm = [None] * NT
                    for tl in range(NT):
                        deps = [pb_reader[tl], ev_gate]
                        if tl == 0:
                            deps += [d_we, ev_x2[3]]
                        for dc in range(NDC):
                            s.op("tensor", lambda e, tl=tl, dc=dc, slot=slot_e: e.matmul(
                                out=pb[tl],
                                lhsT=vslice(lnT[tl][:], 0, 128, dc * 128, [[1, 128]]),
                                rhs=vslice(wbuf[slot][:], 0, 128, dc * 512, [[1, 512]]),
                                start=(dc == 0), stop=(dc == NDC - 1)),
                                deps=deps if dc == 0 else [])
                        ev_mm[tl] = s.sig("tensor")
                        pe_tail = ev_mm[tl]
                    ws.consumed(ev_mm[3])
                    ws.prefetch(mk_loader(scr["down"], oc, "down"))
                    for tl in range(NT):
                        s.op("scalar", lambda e, tl=tl: e.activation(
                            out=vslice(ar2_f[:], 0, 128, tl * 512, [[1, 512]]),
                            in_=pb[tl], func=AF.Identity, scale=g_sb[:, tl:tl + 1]),
                            deps=[ev_mm[tl]])
                        pb_reader[tl] = s.sig("scalar")
                    (slot_d, d_wd) = ws.current()
                    ev_mmd = [None] * NT
                    for tl in range(NT):
                        deps = [pb_reader[tl], ev_hT[tl]]
                        if tl == 0:
                            deps.append(d_wd)
                        for dc in range(NDC):
                            s.op("tensor", lambda e, tl=tl, dc=dc, slot=slot_d: e.matmul(
                                out=pb[tl],
                                lhsT=vslice(ar1_bf[:], 0, 128, tl * 2048 + dc * 128, [[1, 128]]),
                                rhs=vslice(wbuf[slot][:], 0, 128, dc * 512, [[1, 512]]),
                                start=(dc == 0), stop=(dc == NDC - 1)),
                                deps=deps if dc == 0 else [])
                        ev_mmd[tl] = s.sig("tensor")
                        pe_tail = ev_mmd[tl]
                    ws.consumed(ev_mmd[3])
                    if oc < 3:
                        ws.prefetch(mk_loader_flat(oc + 1))
                    for tl in range(NT):
                        s.op("vector", lambda e, tl=tl, oc=oc, stg=stg: e.tensor_tensor(
                            out=vslice(stg[tl], 0, 128, oc * 512, [[1, 512]]),
                            in0=pb[tl],
                            in1=vslice(ar2_f[:], 0, 128, tl * 512, [[1, 512]]),
                            op=ALU.add), deps=[ev_mmd[tl]])
                        ev_ffn[tl][oc] = s.sig("vector")
                        pb_reader[tl] = ev_ffn[tl][oc]

                if dbg and seq == 0 and c == dbgc:
                    for _tl in range(NT):
                        tap(f"ffn{_tl}", stg[_tl], [128, 2048], F32, ev_ffn[_tl][3])
                ev_sto = [ln_stats_only(ffn_rows[tl], [ev_ffn[tl][q] for q in range(4)], tl)
                          for tl in range(NT)]
                ev_fin = [ln_fin(tl, ev_sto[tl]) for tl in range(NT)]
                for tl in range(NT):
                    g = c * NT + tl
                    t0 = g * 128
                    s.op("scalar", lambda e, tl=tl, fr=ffn_rows: e.activation(
                        out=fr[tl], in_=fr[tl], func=AF.Identity,
                        scale=stat[:, 2 * tl:2 * tl + 1], bias=stat[:, 2 * tl + 1:2 * tl + 2]),
                        deps=[ev_fin[tl]])
                    ev_lf = s.sig("scalar")
                    s.op("gpsimd", lambda e, tl=tl, xt=xt, fr=ffn_rows: e.tensor_tensor(
                        out=xt[tl], in0=xt[tl], in1=fr[tl], op=ALU.add), deps=[ev_lf])
                    ev_xf = s.sig("gpsimd")
                    ev_staging_last[tl] = ev_xf
                    ev_x_out_last[tl] = s.dma("sync", lambda e, tl=tl, seq=seq, t0=t0, xt=xt: e.dma_start(
                        out=out_d[seq, t0:t0 + 128, :], in_=xt[tl]), deps=[ev_xf])
                first = False
                first = False

        s.op("sync", lambda e: e.nop(), deps=[("qs", s.count["qs"]), ("qg", s.count["qg"]),
                                              ("qa", s.count["qa"])])

        block = st.enter_context(nc.Block())
        s.emit(block, sems)
    return nc


def _host_prep(inputs):
    wq = np.asarray(inputs["L0_attn_wq"])
    wk = np.asarray(inputs["L0_attn_wk"])
    wv = np.asarray(inputs["L0_attn_wv"])
    wo = np.asarray(inputs["L0_attn_wo"])
    perm_heads = []
    for j in range(4):
        for a in range(4):
            perm_heads += [8 * j + a, 8 * j + 4 + a]
    deint = np.concatenate([np.arange(0, 64, 2), np.arange(1, 64, 2)])
    wq_rows = np.concatenate([h * 64 + deint for h in perm_heads])
    wk_rows = np.concatenate([k * 64 + deint for k in range(8)])
    wo_cols = np.concatenate([np.arange(h * 64, h * 64 + 64) for h in perm_heads])
    i32 = np.arange(0, HD, 2, dtype=np.float32)
    theta = (1.0 / (10000.0 ** (i32 / np.float32(HD)))).astype(np.float32)
    t = np.arange(T, dtype=np.float32)
    freqs = t[:, None] * theta[None, :]
    cos = np.cos(freqs).astype(np.float32)
    sin = np.sin(freqs).astype(np.float32)
    return {
        "wq": np.ascontiguousarray(wq[wq_rows]),
        "wk": np.ascontiguousarray(wk[wk_rows]),
        "wv": np.ascontiguousarray(wv),
        "wo": np.ascontiguousarray(wo[:, wo_cols]),
        "up": np.ascontiguousarray(np.asarray(inputs["L0_ffn_W_up"])),
        "down": np.ascontiguousarray(np.asarray(inputs["L0_ffn_W_down"])),
        "expert": np.ascontiguousarray(np.asarray(inputs["L0_ffn_W_expert"])),
        "gate": np.ascontiguousarray(np.asarray(inputs["L0_ffn_W_gate"])),
        "cosq": cos * np.float32(0.125), "sinq": sin * np.float32(0.125),
        "cosk": cos, "sink": sin,
        "ident": np.eye(128, dtype=np.float32).astype(bfdt),
        "masktile": (np.arange(128)[None, :] >= np.arange(128)[:, None]).astype(bfdt),
    }


def kernel(**inputs):
    x = np.asarray(inputs["L0_attn_x"], dtype=np.float32)
    consts = _host_prep(inputs)
    nc = build_program()
    in_maps = []
    for core in range(8):
        m = dict(consts)
        m["x"] = np.ascontiguousarray(x[2 * core:2 * core + 2])
        in_maps.append(m)
    res = run_bass_kernel_spmd(nc, in_maps, core_ids=list(range(8)))
    out = np.concatenate([np.asarray(r["out"]) for r in res.results], axis=0)
    return out.reshape(B, T, D)


# version 28
# speedup vs baseline: 1.0384x; 1.0384x over previous
import numpy as np
import ml_dtypes
from contextlib import ExitStack

import concourse.bass as bass
import concourse.mybir as mybir
from concourse.bass_utils import run_bass_kernel_spmd

F32 = mybir.dt.float32
BF16 = mybir.dt.bfloat16
AF = mybir.ActivationFunctionType
ALU = mybir.AluOpType
AX = mybir.AxisListType
bfdt = ml_dtypes.bfloat16

B, T, D = 16, 2048, 2048
H, HKV, HD = 32, 8, 64
NSEQ = 2
CH = 512
NT = CH // 128
NCH = T // CH
NDC = D // 128
EPS = 1e-5

ENG = ("tensor", "vector", "scalar", "gpsimd", "sync")
ENGSEM = {"tensor": "pe", "vector": "dve", "scalar": "act", "gpsimd": "pool"}
QSEM = {"sync": "qs", "gpsimd": "qg"}
SEMS = ("pe", "dve", "act", "pool", "qs", "qg")


class Sched:

    def __init__(self, nc):
        self.nc = nc
        self.thunks = {e: [] for e in ENG}
        self.count = {s: 0 for s in SEMS}
        self.hw = {e: {s: 0 for s in SEMS} for e in ENG}

    def _wait(self, eng, sem, val):
        if val <= self.hw[eng][sem]:
            return
        self.hw[eng][sem] = val
        self.thunks[eng].append(("wait", sem, val))

    def _deps(self, eng, deps):
        for d in deps:
            if d is None:
                continue
            if isinstance(d, list):
                self._deps(eng, d)
                continue
            sem, val = d
            self._wait(eng, sem, val)

    def op(self, eng, fn, deps=()):
        self._deps(eng, deps)
        self.thunks[eng].append(("op", fn))

    def sig(self, eng):
        sem = ENGSEM[eng]
        self.count[sem] += 1
        self.thunks[eng].append(("sig", sem))
        return (sem, self.count[sem])

    def dma(self, eng, fn, deps=()):
        qsem = QSEM[eng]
        self._deps(eng, deps)
        self.count[qsem] += 16
        self.thunks[eng].append(("dma", fn, qsem))
        return (qsem, self.count[qsem])

    def emit(self, block, sems):
        secmap = {"sync": block.sync, "tensor": block.tensor, "vector": block.vector,
                  "scalar": block.scalar, "gpsimd": block.gpsimd}
        for e in ENG:
            items = self.thunks[e]
            if not items:
                continue

            def make(items=items):
                def body(eng):
                    last = None
                    for it in items:
                        if it[0] == "wait":
                            eng.wait_ge(sems[it[1]], it[2])
                        elif it[0] == "op":
                            last = it[1](eng)
                        elif it[0] == "sig":
                            last.then_inc(sems[it[1]], 1)
                        else:
                            it[1](eng).then_inc(sems[it[2]], 16)
                return body
            secmap[e](make())


def vslice(base_ap, part0, nparts, off, pat):
    pstep = base_ap.ap[0][0]
    return bass.AP(base_ap.tensor, base_ap.offset + part0 * pstep + off,
                   [[pstep, nparts]] + [list(p) for p in pat])


class WStream:

    def __init__(self, s, wbuf):
        self.s = s
        self.wbuf = wbuf
        self.user_ev = [None, None]
        self.next_slot = 0
        self.q = []

    def prefetch(self, loader, extra_deps=()):
        slot = self.next_slot
        deps = [d for d in extra_deps if d is not None]
        if self.user_ev[slot] is not None:
            deps.append(self.user_ev[slot])
        ev = loader(slot, deps)
        self.q.append((slot, ev))
        self.next_slot = 1 - slot
        return slot, ev

    def current(self):
        return self.q[0]

    def consumed(self, pe_ev):
        slot, _ = self.q.pop(0)
        self.user_ev[slot] = pe_ev


def build_program(nseq=NSEQ, nch=NCH, dbg=False, dbgc=0):
    nc = bass.Bass("TRN2", target_bir_lowering=False)
    dbg_outs = {}

    x_d = nc.dram_tensor("x", [NSEQ, T, D], F32, kind="ExternalInput").ap()
    out_d = nc.dram_tensor("out", [NSEQ, T, D], F32, kind="ExternalOutput").ap()
    w_in = {}
    for name, shp in (("wq", [D, D]), ("wk", [512, D]), ("wv", [512, D]), ("wo", [D, D]),
                      ("up", [D, D]), ("down", [D, D]), ("expert", [D, D]), ("gate", [8, D])):
        w_in[name] = nc.dram_tensor(name, shp, F32, kind="ExternalInput").ap()
    cosq_d = nc.dram_tensor("cosq", [T, 32], F32, kind="ExternalInput").ap()
    sinq_d = nc.dram_tensor("sinq", [T, 32], F32, kind="ExternalInput").ap()
    cosk_d = nc.dram_tensor("cosk", [T, 32], F32, kind="ExternalInput").ap()
    sink_d = nc.dram_tensor("sink", [T, 32], F32, kind="ExternalInput").ap()
    ident_d = nc.dram_tensor("ident", [128, 128], BF16, kind="ExternalInput").ap()
    mask_d = nc.dram_tensor("masktile", [128, 128], BF16, kind="ExternalInput").ap()

    scr = {}
    for name, n_ot in (("wq", 16), ("wo", 16), ("up", 16), ("down", 16), ("wk", 4), ("wv", 4)):
        scr[name] = nc.dram_tensor(name + "T_s", [NDC, 128, n_ot, 128], BF16, kind="Internal").ap()

    with ExitStack() as st:
        def sb(name, shape, dt):
            return st.enter_context(nc.sbuf_tensor(name, shape, dt))

        def psum(name, shape, dt):
            return st.enter_context(nc.psum_tensor(name, shape, dt))

        x_sb = [sb(f"x{i}", [128, D], F32) for i in range(NT)]
        lnT = [sb(f"lnT{i}", [128, D], BF16) for i in range(NT)]
        kT = [sb(f"kT{j}", [128, T], BF16) for j in range(4)]
        v_sb = [sb(f"v{g}", [128, 8 * 65], BF16) for g in range(16)]
        cq_sb = sb("cq", [128, 16 * 32], F32)
        sq_sb = sb("sq", [128, 16 * 32], F32)
        ck_sb = sb("ck", [128, 16 * 32], F32)
        sk_sb = sb("sk", [128, 16 * 32], F32)
        id_bf = sb("idbf", [128, 128], BF16)
        mk_bf = sb("mkbf", [128, 128], BF16)
        gT_sb = sb("gT", [128, NDC * 8], BF16)
        ones_bf = sb("ones", [128, 64], BF16)
        wbuf = [sb(f"wbuf{k}", [128, NDC * 512], BF16) for k in range(2)]
        ar1_bf = sb("ar1bf", [128, 8192], BF16)
        ar2_bf = sb("ar2bf", [128, 8192], BF16)
        ar3_bf = sb("ar3bf", [128, 4096], BF16)
        ar1_f = sb("ar1f", [128, 8192], F32)
        ar2_f = sb("ar2f", [128, 2048], F32)
        rl_bf = sb("rlbf", [128, 1024], BF16)
        stat = sb("stat", [128, 64], F32)
        eps_sb = sb("eps", [128, 1], F32)
        g_sb = sb("gsb", [128, NT], F32)
        dbgv = sb("dbgv", [128, 16], F32)
        dbgv2 = sb("dbgv2", [128, 8], F32)

        psS = psum("psS", [128, 2048], F32)
        psV = psum("psV", [128, 1024], F32)
        psT = psum("psT", [128, 2048], BF16)

        sems = {m: st.enter_context(nc.semaphore(m)) for m in SEMS}
        s = Sched(nc)

        pb = [psS[:, 0:512], psS[:, 512:1024], psS[:, 1024:1536], psS[:, 1536:2048]]
        scores = [psS[:, 0:1024], psS[:, 1024:2048]]

        pb_reader = [None] * 4
        psV_reader = None
        psT_reader = None
        psT_war = [None]
        pe_tail = None
        act_ar3 = None

        ev_consts = []
        for dst, src in ((cq_sb, cosq_d), (sq_sb, sinq_d), (ck_sb, cosk_d), (sk_sb, sink_d)):
            ev_consts.append(s.dma("sync", lambda e, dst=dst, src=src: e.dma_start(
                out=dst[:].rearrange("p (g c) -> p g c", c=32),
                in_=src.rearrange("(g p) c -> p g c", p=128))))
        ev_consts.append(s.dma("sync", lambda e: e.dma_start(out=id_bf[:], in_=ident_d)))
        ev_consts.append(s.dma("sync", lambda e: e.dma_start(out=mk_bf[:], in_=mask_d)))
        s.op("vector", lambda e: e.memset(ones_bf[:], 1.0))
        ev_ones = s.sig("vector")
        s.op("vector", lambda e: e.memset(eps_sb[:], EPS))
        ev_eps = s.sig("vector")
        for g in range(16):
            s.op("gpsimd", lambda e, g=g: e.memset(
                vslice(v_sb[g][:], 0, 128, 64, [[65, 8], [1, 1]]), 1.0))
        ev_vones = s.sig("gpsimd")

        ev_wprep = []

        prep_state = {"tp": [None, None], "out": [None, None], "ot": 0}

        def prep_T(name):
            W, WT = w_in[name], scr[name]
            n_ot = WT.tensor.shape[2]
            nonlocal psT_reader, pe_tail
            ev_tp = prep_state["tp"]
            ev_out = prep_state["out"]
            for ot_l in range(n_ot):
                ot = ot_l
                sl = (prep_state["ot"] + ot_l) % 2
                bft = ar3_bf[:, sl * 2048:sl * 2048 + 2048]
                d_in = s.dma("gpsimd", lambda e, W=W, ot=ot, bft=bft: e.dma_start(
                    out=bft, in_=W[ot * 128:(ot + 1) * 128, :]),
                    deps=[ev_tp[sl]])
                ev_ev = None
                stg = ar2_bf[:, sl * 2048:sl * 2048 + 2048]
                for k in range(16):
                    deps = [d_in, ev_consts[4]]
                    if psT_reader:
                        deps.append(psT_reader)
                    s.op("tensor", lambda e, k=k, bft=bft: e.transpose(
                        out=psT[:, k * 128:(k + 1) * 128],
                        in_=vslice(bft, 0, 128, k * 128, [[1, 128]]),
                        identity=id_bf[:]), deps=deps if k == 0 else [])
                ev_t = s.sig("tensor")
                pe_tail = ev_t
                s.op("vector", lambda e, stg=stg: e.tensor_copy(out=stg, in_=psT[:]),
                     deps=[ev_t, ev_out[sl]])
                psT_reader = s.sig("vector")
                ev_tp[sl] = pe_tail
                ev_out[sl] = s.dma("sync", lambda e, WT=WT, ot=ot, stg=stg: e.dma_start(
                    out=WT[:, :, ot, :].rearrange("dc p o -> p dc o"),
                    in_=stg.rearrange("p (dc o) -> p dc o", o=128)),
                    deps=[psT_reader])
            prep_state["ot"] += n_ot
            ev_wprep.append(ev_out[0])
            ev_wprep.append(ev_out[1])

        for nm in ("wk", "wv", "wq", "wo", "up", "down"):
            prep_T(nm)

        d_in = s.dma("gpsimd", lambda e: e.dma_start(out=ar3_bf[0:8, 0:2048], in_=w_in["gate"]),
                     deps=[pe_tail])
        for dc in range(NDC):
            deps = [d_in] + ([psT_reader] if psT_reader else [])
            s.op("tensor", lambda e, dc=dc: e.transpose(
                out=psT[:, dc * 8:(dc + 1) * 8],
                in_=vslice(ar3_bf[0:8, 0:2048], 0, 8, dc * 128, [[1, 128]]),
                identity=id_bf[0:8, 0:8]), deps=deps if dc == 0 else [])
        ev_t = s.sig("tensor")
        pe_tail = ev_t
        s.op("vector", lambda e: e.tensor_copy(out=gT_sb[:], in_=psT[:, 0:128]), deps=[ev_t])
        psT_reader = s.sig("vector")
        ev_gT = psT_reader

        def ln_stats_only(tile_ap, dep_evs, tl):
            for q in range(4):
                s.op("vector", lambda e, q=q: e.bn_stats(
                    out=stat[:, 32 + 6 * q:38 + 6 * q],
                    in_=vslice(tile_ap, 0, 128, q * 512, [[1, 512]])),
                    deps=dep_evs if q == 0 else [])
            ev_bn = s.sig("vector")
            s.op("vector", lambda e, tl=tl: e.bn_aggr(
                out=stat[:, 8 + 2 * tl:10 + 2 * tl], in_=stat[:, 32:56]), deps=[ev_bn])
            return s.sig("vector")

        def ln_fin(tl, dep):
            s.op("scalar", lambda e, tl=tl: e.activation(
                out=stat[:, 56 + tl:57 + tl], in_=stat[:, 9 + 2 * tl:10 + 2 * tl],
                func=AF.Ln, bias=eps_sb[:]), deps=[dep, ev_eps])
            ev_lnv = s.sig("scalar")
            s.op("scalar", lambda e, tl=tl: e.activation(
                out=stat[:, 2 * tl:2 * tl + 1], in_=stat[:, 56 + tl:57 + tl],
                func=AF.Exp, scale=-0.5), deps=[ev_lnv])
            ev_rstd = s.sig("scalar")
            s.op("vector", lambda e, tl=tl: e.tensor_scalar(
                out=stat[:, 2 * tl + 1:2 * tl + 2], in0=stat[:, 8 + 2 * tl:9 + 2 * tl],
                scalar1=stat[:, 2 * tl:2 * tl + 1], scalar2=-1.0,
                op0=ALU.mult, op1=ALU.mult), deps=[ev_rstd])
            return s.sig("vector")

        def ln_stats(tile_ap, dep_evs, tl=0):
            return ln_fin(tl, ln_stats_only(tile_ap, dep_evs, tl))

        def transpose16(src_bf, dst_ap, dep_evs):
            nonlocal psT_reader, pe_tail
            for k in range(16):
                deps = list(dep_evs) + ([psT_reader] if psT_reader else []) + [psT_war[0]]
                s.op("tensor", lambda e, k=k: e.transpose(
                    out=psT[:, k * 128:(k + 1) * 128],
                    in_=vslice(src_bf, 0, 128, k * 128, [[1, 128]]),
                    identity=id_bf[:]), deps=deps if k == 0 else [])
            pe_ev = s.sig("tensor")
            pe_tail = pe_ev
            s.op("vector", lambda e: e.tensor_copy(out=dst_ap, in_=psT[:]), deps=[pe_ev])
            psT_reader = s.sig("vector")
            return psT_reader

        def rope6(ps_src, nblk, cos_t, sin_t, gtile, out_bf, dep_evs):
            cs = vslice(cos_t[:], 0, 128, gtile * 32, [[0, nblk], [1, 32]])
            sn = vslice(sin_t[:], 0, 128, gtile * 32, [[0, nblk], [1, 32]])
            a_v = vslice(ps_src, 0, 128, 0, [[64, nblk], [1, 32]])
            b_v = vslice(ps_src, 0, 128, 32, [[64, nblk], [1, 32]])
            t1 = vslice(ar2_f[:], 0, 128, 0, [[32, nblk], [1, 32]])
            t2 = vslice(ar2_f[:], 0, 128, 512, [[32, nblk], [1, 32]])
            oa = vslice(out_bf, 0, 128, 0, [[64, nblk], [1, 32]])
            ob = vslice(out_bf, 0, 128, 32, [[64, nblk], [1, 32]])
            s.op("vector", lambda e: e.tensor_tensor(out=t1, in0=a_v, in1=cs, op=ALU.mult),
                 deps=dep_evs)
            s.op("vector", lambda e: e.tensor_tensor(out=t2, in0=b_v, in1=sn, op=ALU.mult))
            ev_r1 = s.sig("vector")
            s.op("vector", lambda e: e.tensor_tensor(out=oa, in0=t1, in1=t2, op=ALU.subtract),
                 deps=[ev_r1])
            s.op("vector", lambda e: e.tensor_tensor(out=t1, in0=a_v, in1=sn, op=ALU.mult))
            s.op("vector", lambda e: e.tensor_tensor(out=t2, in0=b_v, in1=cs, op=ALU.mult))
            ev_r2 = s.sig("vector")
            s.op("vector", lambda e: e.tensor_tensor(out=ob, in0=t1, in1=t2, op=ALU.add),
                 deps=[ev_r2])
            return s.sig("vector")

        def mm_proj(bank_i, lhsT_src, lhs_off, rhs_buf, dc, start, stop):
            return lambda e: e.matmul(
                out=pb[bank_i],
                lhsT=vslice(lhsT_src, 0, 128, lhs_off + dc * 128, [[1, 128]]),
                rhs=vslice(rhs_buf[:], 0, 128, dc * 512, [[1, 512]]),
                start=start, stop=stop)

        ev_out_dma = {}
        barrier = list(ev_wprep) + ev_consts + [ev_ones, ev_vones, ev_gT]
        ws = WStream(s, wbuf)

        def mk_loader(scr_ap, oc):
            def loader(slot, deps):
                return s.dma("sync", lambda e: e.dma_start(
                    out=wbuf[slot][:].rearrange("p (dc o) -> p dc o", dc=NDC),
                    in_=scr_ap[:, :, oc * 4:(oc + 1) * 4, :].rearrange("dc p ot o -> p dc (ot o)")),
                    deps=deps)
            return loader

        def mk_loader_small(scr_ap):
            def loader(slot, deps):
                return s.dma("sync", lambda e: e.dma_start(
                    out=wbuf[slot][:].rearrange("p (dc o) -> p dc o", dc=NDC),
                    in_=scr_ap[:, :, :, :].rearrange("dc p ot o -> p dc (ot o)")),
                    deps=deps)
            return loader

        def mk_loader_flat(oc):
            def loader(slot, deps):
                return s.dma("gpsimd", lambda e: e.dma_start(
                    out=wbuf[slot][:].rearrange("p (dc o) -> p dc o", dc=NDC),
                    in_=w_in["expert"].rearrange("(dc p) o -> p dc o", p=128)[:, :, oc * 512:(oc + 1) * 512]),
                    deps=deps)
            return loader

        def tap(name, src_ap, shape, dt, ev):
            if not dbg:
                return None
            dout = nc.dram_tensor("dbg_" + name, shape, dt, kind="ExternalOutput").ap()
            dbg_outs[name] = dout
            return s.dma("sync", lambda e: e.dma_start(out=dout, in_=src_ap), deps=[ev])

        xA = [x_sb[i][:] for i in range(NT)]
        xB = [vslice(ar1_f[:], 0, 128, i * 2048, [[1, 2048]]) for i in range(NT)]
        ev_staging_last = [None] * NT
        ln_slot_war = [None, None]
        ev_x_out_last = [None] * NT

        first = True
        cg = -1
        for seq in range(nseq):
            for c in range(nch):
                cg += 1
                xt = xA if cg % 2 == 0 else xB
                stg = xB if cg % 2 == 0 else xA
                dep0 = barrier if first else []
                ws.prefetch(mk_loader_small(scr["wk"]), dep0)
                ws.prefetch(mk_loader_small(scr["wv"]), dep0)

                ev_lnT = [None] * NT
                ev_kv_pe = None
                ev_v_evac = None
                for i in range(NT):
                    g = c * NT + i
                    t0 = g * 128
                    depx = dep0 + [ev_staging_last[i]]
                    d_x = s.dma("gpsimd", lambda e, i=i, t0=t0, seq=seq, xt=xt: e.dma_start(
                        out=xt[i], in_=x_d[seq, t0:t0 + 128, :]), deps=depx)
                    ev_st = ln_stats(xt[i], [d_x], i)
                    sl1 = i % 2
                    ln1 = ar3_bf[:, sl1 * 2048:sl1 * 2048 + 2048]
                    dep_ln = [ev_st, ln_slot_war[sl1], act_ar3]
                    s.op("scalar", lambda e, i=i, ln1=ln1, xt=xt: e.activation(
                        out=ln1, in_=xt[i], func=AF.Identity,
                        scale=stat[:, 2 * i:2 * i + 1], bias=stat[:, 2 * i + 1:2 * i + 2]),
                        deps=dep_ln)
                    ev_ln1 = s.sig("scalar")
                    if dbg and seq == 0 and c == dbgc and i == 0:
                        s.op("vector", lambda e: e.tensor_copy(out=dbgv2[:], in_=stat[:, 0:8]))
                        ev_d2 = s.sig("vector")
                        tap("lnstat0", dbgv2[:], [128, 8], F32, ev_d2)
                    ev_lnT[i] = transpose16(ln1, lnT[i][:], [ev_ln1])
                    ln_slot_war[sl1] = pe_tail
                    (slk, d_wk) = ws.q[0]
                    (slv, d_wv) = ws.q[1]
                    for dc in range(NDC):
                        deps = [ev_lnT[i], d_wk, d_wv, psV_reader] if dc == 0 else []
                        s.op("tensor", lambda e, i=i, dc=dc, slk=slk: e.matmul(
                            out=psV[:, 0:512],
                            lhsT=vslice(lnT[i][:], 0, 128, dc * 128, [[1, 128]]),
                            rhs=vslice(wbuf[slk][:], 0, 128, dc * 512, [[1, 512]]),
                            start=(dc == 0), stop=(dc == NDC - 1)), deps=deps)
                        s.op("tensor", lambda e, i=i, dc=dc, slv=slv: e.matmul(
                            out=psV[:, 512:1024],
                            lhsT=vslice(lnT[i][:], 0, 128, dc * 128, [[1, 128]]),
                            rhs=vslice(wbuf[slv][:], 0, 128, dc * 512, [[1, 512]]),
                            start=(dc == 0), stop=(dc == NDC - 1)))
                    ev_kvmm = s.sig("tensor")
                    pe_tail = ev_kvmm
                    ev_kv_pe = ev_kvmm
                    krope = ar2_bf[:, (i % 2) * 512:(i % 2) * 512 + 512]
                    ev_krope = rope6(psV[:, 0:512], 8, ck_sb, sk_sb, g, krope, [ev_kvmm])
                    for j in range(4):
                        deps = [ev_krope] + ([psT_reader] if psT_reader else [])
                        s.op("tensor", lambda e, j=j, krope=krope: e.transpose(
                            out=psT[:, j * 128:(j + 1) * 128],
                            in_=vslice(krope, 0, 128, j * 128, [[1, 128]]),
                            identity=id_bf[:]), deps=deps if j == 0 else [])
                    ev_kt = s.sig("tensor")
                    pe_tail = ev_kt
                    for j in range(4):
                        s.op("vector", lambda e, j=j, g=g: e.tensor_copy(
                            out=kT[j][:, g * 128:(g + 1) * 128],
                            in_=psT[:, j * 128:(j + 1) * 128]), deps=[ev_kt] if j == 0 else [])
                    psT_reader = s.sig("vector")
                    s.op("scalar", lambda e, g=g: e.activation(
                        out=vslice(v_sb[g][:], 0, 128, 0, [[65, 8], [1, 64]]),
                        in_=vslice(psV[:], 0, 128, 512, [[64, 8], [1, 64]]),
                        func=AF.Copy), deps=[ev_kvmm, ev_vones])
                    ev_v_evac = s.sig("scalar")
                    psV_reader = ev_v_evac
                ev_kT_ready = psT_reader
                ws.consumed(ev_kv_pe)
                ws.consumed(ev_kv_pe)
                if dbg and seq == 0 and c == dbgc:
                    tap("lnT0", lnT[0][:], [128, 2048], BF16, ev_lnT[0])
                    tap("kT0", kT[0][:, 0:512], [128, 512], BF16, ev_kT_ready)
                    tap("v0", v_sb[0][:], [128, 520], BF16, ev_v_evac)

                ws.prefetch(mk_loader(scr["wq"], 0))
                qT_base = ar1_bf[:]
                ev_qT = None
                ev_qt_prev = None
                for oc in range(4):
                    (slot, d_w) = ws.current()
                    ev_mm = [None] * NT
                    for tl in range(NT):
                        deps = []
                        if tl == 0:
                            deps = [d_w] + [ev_lnT[t2_] for t2_ in range(NT)]
                        deps.append(pb_reader[tl])
                        for dc in range(NDC):
                            s.op("tensor", mm_proj(tl, lnT[tl][:], 0, wbuf[slot], dc,
                                                   dc == 0, dc == NDC - 1),
                                 deps=deps if dc == 0 else [])
                        ev_mm[tl] = s.sig("tensor")
                        pe_tail = ev_mm[tl]
                    ws.consumed(ev_mm[3])
                    if oc < 3:
                        ws.prefetch(mk_loader(scr["wq"], oc + 1))
                    else:
                        ws.prefetch(mk_loader(scr["wo"], 0))
                    for tl in range(NT):
                        g = c * NT + tl
                        qrope = ar2_bf[:, 1024 + (tl % 2) * 512:1024 + (tl % 2) * 512 + 512]
                        ev_qr = rope6(pb[tl], 8, cq_sb, sq_sb, g, qrope,
                                      [ev_mm[tl], ev_qt_prev])
                        pb_reader[tl] = ev_qr
                        for ci in range(4):
                            deps = [ev_qr] + ([psT_reader] if psT_reader else [])
                            s.op("tensor", lambda e, ci=ci, qrope=qrope: e.transpose(
                                out=psT[:, ci * 128:(ci + 1) * 128],
                                in_=vslice(qrope, 0, 128, ci * 128, [[1, 128]]),
                                identity=id_bf[:]), deps=deps if ci == 0 else [])
                        ev_qt = s.sig("tensor")
                        pe_tail = ev_qt
                        ev_qt_prev = ev_qt
                        for ci in range(4):
                            cp = oc * 4 + ci
                            s.op("vector", lambda e, cp=cp, tl=tl, ci=ci: e.tensor_copy(
                                out=vslice(qT_base, 0, 128, cp * 512 + tl * 128, [[1, 128]]),
                                in_=psT[:, ci * 128:(ci + 1) * 128]),
                                deps=[ev_qt] if ci == 0 else [])
                        psT_reader = s.sig("vector")
                ev_qT = psT_reader
                if dbg and seq == 0 and c == dbgc:
                    tap("qT", ar1_bf[:], [128, 8192], BF16, ev_qT)

                nkt = (c + 1) * NT
                ev_exp_slot = [None, None]
                ev_av_slot = [None, None]
                ev_yT = []
                pending_epi = None
                ev_rl_pe = None
                av_evac_ev = [None, None]
                for p in range(16):
                    j = p // 4
                    kvA, kvB = 2 * j, 2 * j + 1
                    ev_avlast = None
                    pending_av = None
                    for ktc in range(nkt):
                        sl = ktc % 2
                        diag = ktc >= c * NT
                        qt_off = 128 * (ktc - c * NT) if diag else 0
                        N = 512 - qt_off
                        deps = [ev_qT, ev_kT_ready, ev_exp_slot[sl]]
                        s.op("tensor", lambda e, j=j, p=p, ktc=ktc, sl=sl, qt_off=qt_off, N=N: e.matmul(
                            out=psS[:, sl * 1024 + qt_off:sl * 1024 + qt_off + N],
                            lhsT=kT[j][0:64, ktc * 128:(ktc + 1) * 128],
                            rhs=vslice(qT_base, 0, 64, p * 512 + qt_off, [[1, N]]),
                            start=True, stop=True), deps=deps)
                        s.op("tensor", lambda e, j=j, p=p, ktc=ktc, sl=sl, qt_off=qt_off, N=N: e.matmul(
                            out=psS[:, sl * 1024 + 512 + qt_off:sl * 1024 + 512 + qt_off + N],
                            lhsT=kT[j][64:128, ktc * 128:(ktc + 1) * 128],
                            rhs=vslice(qT_base, 64, 64, p * 512 + qt_off, [[1, N]]),
                            start=True, stop=True))
                        ev_sc = s.sig("tensor")
                        pe_tail = ev_sc
                        if pending_av is not None:
                            pending_av()
                            pending_av = None
                        deps_e = [ev_sc, ev_av_slot[sl]]
                        s.op("scalar", lambda e, sl=sl, qt_off=qt_off, N=N: e.activation(
                            out=vslice(ar3_bf[:], 0, 128, sl * 1024 + qt_off, [[512, 2], [1, N]]),
                            in_=vslice(psS[:], 0, 128, sl * 1024 + qt_off, [[512, 2], [1, N]]),
                            func=AF.Exp), deps=deps_e)
                        ev_exp = s.sig("scalar")
                        act_ar3 = ev_exp

                        def make_av(ktc=ktc, sl=sl, diag=diag, qt_off=qt_off, N=N,
                                    ev_exp=ev_exp, kvA=kvA, kvB=kvB, nkt=nkt):
                            def do_av():
                                nonlocal ev_avlast, pe_tail
                                ev_mask = None
                                if diag:
                                    for half in range(2):
                                        s.op("vector", lambda e, sl=sl, qt_off=qt_off, half=half: e.tensor_tensor(
                                            out=vslice(ar3_bf[:], 0, 128, sl * 1024 + half * 512 + qt_off, [[1, 128]]),
                                            in0=vslice(ar3_bf[:], 0, 128, sl * 1024 + half * 512 + qt_off, [[1, 128]]),
                                            in1=mk_bf[:], op=ALU.mult),
                                            deps=[ev_exp, ev_consts[5]] if half == 0 else [])
                                    ev_mask = s.sig("vector")
                                deps_av = [ev_exp, ev_mask]
                                if ktc == 0:
                                    deps_av += [psV_reader, ev_vones]
                                s.op("tensor", lambda e, kvA=kvA, ktc=ktc, sl=sl, qt_off=qt_off, N=N, nkt=nkt: e.matmul(
                                    out=psV[0:65, qt_off:qt_off + N],
                                    lhsT=v_sb[ktc][:, kvA * 65:(kvA + 1) * 65],
                                    rhs=vslice(ar3_bf[:], 0, 128, sl * 1024 + qt_off, [[1, N]]),
                                    start=(ktc == 0), stop=(ktc == nkt - 1)), deps=deps_av)
                                s.op("tensor", lambda e, kvB=kvB, ktc=ktc, sl=sl, qt_off=qt_off, N=N, nkt=nkt: e.matmul(
                                    out=psV[0:65, 512 + qt_off:512 + qt_off + N],
                                    lhsT=v_sb[ktc][:, kvB * 65:(kvB + 1) * 65],
                                    rhs=vslice(ar3_bf[:], 0, 128, sl * 1024 + 512 + qt_off, [[1, N]]),
                                    start=(ktc == 0), stop=(ktc == nkt - 1)))
                                ev_av = s.sig("tensor")
                                pe_tail = ev_av
                                ev_av_slot[sl] = ev_av
                                ev_avlast = ev_av
                            return do_av
                        ev_exp_slot[sl] = ev_exp
                        pending_av = make_av()
                        if ktc == 2 and pending_epi is not None:
                            pending_epi()
                            pending_epi = None
                    if pending_av is not None:
                        pending_av()
                        pending_av = None
                    aslot = p % 2
                    dep_c = [ev_avlast, av_evac_ev[aslot]]
                    s.op("vector", lambda e, aslot=aslot: e.tensor_copy(
                        out=vslice(ar2_f[:], 0, 65, aslot * 1024, [[1, 1024]]),
                        in_=psV[0:65, 0:1024]), deps=dep_c)
                    ev_avc = s.sig("vector")
                    psV_reader = ev_avc

                    def make_epi(p=p, aslot=aslot, ev_avc=ev_avc):
                        def epi():
                            nonlocal ev_rl_pe, pe_tail
                            s.op("scalar", lambda e: e.activation(
                                out=vslice(ar2_f[:], 64, 1, aslot * 1024, [[1, 1024]]),
                                in_=vslice(ar2_f[:], 64, 1, aslot * 1024, [[1, 1024]]),
                                func=AF.Ln), deps=[ev_avc])
                            ev_lnl = s.sig("scalar")
                            dep_cast = [ev_lnl, ev_rl_pe]
                            s.op("scalar", lambda e: e.activation(
                                out=rl_bf[64:65, 0:1024],
                                in_=vslice(ar2_f[:], 64, 1, aslot * 1024, [[1, 1024]]),
                                func=AF.Exp, scale=-1.0), deps=dep_cast)
                            ev_rlb = s.sig("scalar")
                            bco = psT[:, 1024:2048].bitcast(F32)
                            dep_bc = [ev_rlb, ev_ones, psT_war[0]]
                            s.op("tensor", lambda e, bco=bco: e.matmul(
                                out=vslice(bco, 0, 64, 0, [[1, 512]]),
                                lhsT=ones_bf[64:65, :], rhs=rl_bf[64:65, 0:512],
                                start=True, stop=True), deps=dep_bc)
                            ev_bcA = s.sig("tensor")
                            s.op("vector", lambda e, bco=bco: e.tensor_copy(
                                out=vslice(ar3_bf[:], 0, 64, 2048, [[1, 512]]),
                                in_=vslice(bco, 0, 64, 0, [[1, 512]])), deps=[ev_bcA])
                            ev_evA = s.sig("vector")
                            s.op("tensor", lambda e, bco=bco: e.matmul(
                                out=vslice(bco, 0, 64, 0, [[1, 512]]),
                                lhsT=ones_bf[64:65, :], rhs=rl_bf[64:65, 512:1024],
                                start=True, stop=True), deps=[ev_evA])
                            ev_bcB = s.sig("tensor")
                            pe_tail = ev_bcB
                            ev_rl_pe = ev_bcB
                            s.op("vector", lambda e, bco=bco: e.tensor_copy(
                                out=vslice(ar3_bf[:], 0, 64, 2560, [[1, 512]]),
                                in_=vslice(bco, 0, 64, 0, [[1, 512]])), deps=[ev_bcB])
                            ev_rbc = s.sig("vector")
                            psT_war[0] = ev_rbc
                            s.op("vector", lambda e, p=p: e.tensor_tensor(
                                out=vslice(ar2_bf[:], 0, 64, p * 512, [[1, 512]]),
                                in0=vslice(ar2_f[:], 0, 64, aslot * 1024, [[1, 512]]),
                                in1=vslice(ar3_bf[:], 0, 64, 2048, [[1, 512]]),
                                op=ALU.mult), deps=[ev_rbc])
                            s.op("vector", lambda e, p=p: e.tensor_tensor(
                                out=vslice(ar2_bf[:], 64, 64, p * 512, [[1, 512]]),
                                in0=vslice(ar2_f[:], 0, 64, aslot * 1024 + 512, [[1, 512]]),
                                in1=vslice(ar3_bf[:], 0, 64, 2560, [[1, 512]]),
                                op=ALU.mult))
                            ev_y = s.sig("vector")
                            ev_yT.append(ev_y)
                            av_evac_ev[aslot] = ev_y
                        return epi
                    pending_epi = make_epi()
                if pending_epi is not None:
                    pending_epi()
                    pending_epi = None

                attn_rows = stg
                ev_evac = [[None] * 4 for _ in range(NT)]
                for oc in range(4):
                    (slot, d_w) = ws.current()
                    ev_mm = [None] * NT
                    for tl in range(NT):
                        deps = [pb_reader[tl]]
                        if tl == 0:
                            deps += [d_w, ev_yT[-1]]
                        for dyc in range(NDC):
                            s.op("tensor", lambda e, tl=tl, dyc=dyc, slot=slot: e.matmul(
                                out=pb[tl],
                                lhsT=vslice(ar2_bf[:], 0, 128, dyc * 512 + tl * 128, [[1, 128]]),
                                rhs=vslice(wbuf[slot][:], 0, 128, dyc * 512, [[1, 512]]),
                                start=(dyc == 0), stop=(dyc == NDC - 1)),
                                deps=deps if dyc == 0 else [])
                        ev_mm[tl] = s.sig("tensor")
                        pe_tail = ev_mm[tl]
                    ws.consumed(ev_mm[3])
                    if oc < 3:
                        ws.prefetch(mk_loader(scr["wo"], oc + 1))
                    else:
                        ws.prefetch(mk_loader(scr["up"], 0))
                    for tl in range(NT):
                        dep_ev = [ev_mm[tl]]
                        if oc == 0:
                            dep_ev.append(ev_x_out_last[tl])
                        s.op("scalar", lambda e, tl=tl, oc=oc, stg=stg: e.activation(
                            out=vslice(stg[tl], 0, 128, oc * 512, [[1, 512]]),
                            in_=pb[tl], func=AF.Copy), deps=dep_ev)
                        ev_evac[tl][oc] = s.sig("scalar")
                        pb_reader[tl] = ev_evac[tl][oc]
                if dbg and seq == 0 and c == dbgc:
                    for _tl in range(NT):
                        tap(f"attn{_tl}", stg[_tl], [128, 2048], F32, ev_evac[_tl][3])
                ev_x2 = [None] * NT
                ev_sto = [None] * NT
                for tl in range(NT):
                    ev_sto[tl] = ln_stats_only(attn_rows[tl], [ev_evac[tl][q] for q in range(4)], tl)
                ev_fin = [ln_fin(tl, ev_sto[tl]) for tl in range(NT)]
                ev_ap = [None] * NT
                for tl in range(NT):
                    s.op("scalar", lambda e, tl=tl, ar=attn_rows: e.activation(
                        out=ar[tl], in_=ar[tl], func=AF.Identity,
                        scale=stat[:, 2 * tl:2 * tl + 1], bias=stat[:, 2 * tl + 1:2 * tl + 2]),
                        deps=[ev_fin[tl]])
                    ev_ap[tl] = s.sig("scalar")
                for tl in range(NT):
                    s.op("gpsimd", lambda e, tl=tl, xt=xt, ar=attn_rows: e.tensor_tensor(
                        out=xt[tl], in0=xt[tl], in1=ar[tl], op=ALU.add), deps=[ev_ap[tl]])
                    ev_x2[tl] = s.sig("gpsimd")
                if dbg and seq == 0 and c == dbgc:
                    tap("x2", xt[0], [128, 2048], F32, ev_x2[0])

                ev_ln2T = [None] * NT
                ev_gate = None
                ev_sto = [ln_stats_only(xt[tl], [ev_x2[tl]], tl) for tl in range(NT)]
                ev_fin = [ln_fin(tl, ev_sto[tl]) for tl in range(NT)]
                for tl in range(NT):
                    sl2 = tl % 2
                    ln2 = ar3_bf[:, sl2 * 2048:sl2 * 2048 + 2048]
                    dep_ln = [ev_fin[tl], ln_slot_war[sl2], act_ar3]
                    s.op("scalar", lambda e, ln2=ln2, tl=tl, xt=xt: e.activation(
                        out=ln2, in_=xt[tl], func=AF.Identity,
                        scale=stat[:, 2 * tl:2 * tl + 1], bias=stat[:, 2 * tl + 1:2 * tl + 2]),
                        deps=dep_ln)
                    ev_ln2 = s.sig("scalar")
                    ev_ln2T[tl] = transpose16(ln2, lnT[tl][:], [ev_ln2])
                    ln_slot_war[sl2] = pe_tail
                    for dc in range(NDC):
                        deps = [ev_ln2T[tl], ev_gT, psV_reader] if dc == 0 else []
                        s.op("tensor", lambda e, tl=tl, dc=dc: e.matmul(
                            out=psV[:, 0:8],
                            lhsT=vslice(lnT[tl][:], 0, 128, dc * 128, [[1, 128]]),
                            rhs=gT_sb[:, dc * 8:(dc + 1) * 8],
                            start=(dc == 0), stop=(dc == NDC - 1)), deps=deps)
                    ev_gmm = s.sig("tensor")
                    pe_tail = ev_gmm
                    s.op("vector", lambda e: e.tensor_reduce(
                        out=stat[:, 24:25], in_=psV[:, 0:8], axis=AX.X, op=ALU.max), deps=[ev_gmm])
                    ev_g1 = s.sig("vector")
                    s.op("vector", lambda e: e.tensor_scalar(
                        out=stat[:, 32:40], in0=psV[:, 0:8], scalar1=stat[:, 24:25], scalar2=None,
                        op0=ALU.is_ge, op1=ALU.bypass), deps=[ev_g1])
                    ev_g2 = s.sig("vector")
                    s.op("vector", lambda e: e.scalar_tensor_tensor(
                        out=stat[:, 40:48], in0=stat[:, 32:40], scalar=-1e30, in1=psV[:, 0:8],
                        op0=ALU.mult, op1=ALU.add), deps=[ev_g2])
                    ev_g3 = s.sig("vector")
                    s.op("vector", lambda e: e.tensor_reduce(
                        out=stat[:, 25:26], in_=stat[:, 40:48], axis=AX.X, op=ALU.max),
                        deps=[ev_g3])
                    ev_g4 = s.sig("vector")
                    s.op("vector", lambda e: e.tensor_tensor(
                        out=stat[:, 26:27], in0=stat[:, 25:26], in1=stat[:, 24:25], op=ALU.subtract),
                        deps=[ev_g4])
                    ev_gd = s.sig("vector")
                    s.op("scalar", lambda e: e.activation(
                        out=stat[:, 27:28], in_=stat[:, 26:27], func=AF.Exp), deps=[ev_gd])
                    ev_gu = s.sig("scalar")
                    s.op("vector", lambda e: e.tensor_scalar(
                        out=stat[:, 28:29], in0=stat[:, 27:28], scalar1=1.0, scalar2=None,
                        op0=ALU.add, op1=ALU.bypass), deps=[ev_gu])
                    ev_g5 = s.sig("vector")
                    s.op("vector", lambda e: e.reciprocal(out=stat[:, 29:30], in_=stat[:, 28:29]),
                         deps=[ev_g5])
                    ev_g6 = s.sig("vector")
                    s.op("vector", lambda e, tl=tl: e.scalar_tensor_tensor(
                        out=g_sb[:, tl:tl + 1], in0=stat[:, 27:28], scalar=stat[:, 29:30],
                        in1=stat[:, 29:30], op0=ALU.mult, op1=ALU.add), deps=[ev_g6])
                    ev_gate = s.sig("vector")
                    psV_reader = ev_gate

                if dbg and seq == 0 and c == dbgc:
                    tap("g", g_sb[:], [128, NT], F32, ev_gate)
                ev_h = [[None] * 4 for _ in range(NT)]
                for oc in range(4):
                    (slot, d_w) = ws.current()
                    ev_mm = [None] * NT
                    for tl in range(NT):
                        deps = [pb_reader[tl], ev_ln2T[tl]]
                        if tl == 0:
                            deps.append(d_w)
                        for dc in range(NDC):
                            s.op("tensor", lambda e, tl=tl, dc=dc, slot=slot: e.matmul(
                                out=pb[tl],
                                lhsT=vslice(lnT[tl][:], 0, 128, dc * 128, [[1, 128]]),
                                rhs=vslice(wbuf[slot][:], 0, 128, dc * 512, [[1, 512]]),
                                start=(dc == 0), stop=(dc == NDC - 1)),
                                deps=deps if dc == 0 else [])
                        ev_mm[tl] = s.sig("tensor")
                        pe_tail = ev_mm[tl]
                    ws.consumed(ev_mm[3])
                    if oc < 3:
                        ws.prefetch(mk_loader(scr["up"], oc + 1))
                    else:
                        ws.prefetch(mk_loader_flat(0))
                    for tl in range(NT):
                        s.op("scalar", lambda e, tl=tl, oc=oc: e.activation(
                            out=vslice(ar2_bf[:], 0, 128, tl * 2048 + oc * 512, [[1, 512]]),
                            in_=pb[tl], func=AF.Relu), deps=[ev_mm[tl]])
                        ev_h[tl][oc] = s.sig("scalar")
                        pb_reader[tl] = ev_h[tl][oc]

                if dbg and seq == 0 and c == dbgc:
                    tap("h", ar2_bf[:], [128, 8192], BF16, ev_h[3][3])
                ev_hT = [None] * NT
                for tl in range(NT):
                    src = vslice(ar2_bf[:], 0, 128, tl * 2048, [[1, 2048]])
                    ev_hT[tl] = transpose16(src, vslice(ar1_bf[:], 0, 128, tl * 2048, [[1, 2048]]),
                                            [ev_h[tl][q] for q in range(4)])

                ffn_rows = stg
                ev_ffn = [[None] * 4 for _ in range(NT)]
                for oc in range(4):
                    (slot_e, d_we) = ws.current()
                    ev_mm = [None] * NT
                    for tl in range(NT):
                        deps = [pb_reader[tl], ev_gate]
                        if tl == 0:
                            deps += [d_we, ev_x2[3]]
                        for dc in range(NDC):
                            s.op("tensor", lambda e, tl=tl, dc=dc, slot=slot_e: e.matmul(
                                out=pb[tl],
                                lhsT=vslice(lnT[tl][:], 0, 128, dc * 128, [[1, 128]]),
                                rhs=vslice(wbuf[slot][:], 0, 128, dc * 512, [[1, 512]]),
                                start=(dc == 0), stop=(dc == NDC - 1)),
                                deps=deps if dc == 0 else [])
                        ev_mm[tl] = s.sig("tensor")
                        pe_tail = ev_mm[tl]
                    ws.consumed(ev_mm[3])
                    ws.prefetch(mk_loader(scr["down"], oc))
                    for tl in range(NT):
                        s.op("scalar", lambda e, tl=tl: e.activation(
                            out=vslice(ar2_f[:], 0, 128, tl * 512, [[1, 512]]),
                            in_=pb[tl], func=AF.Identity, scale=g_sb[:, tl:tl + 1]),
                            deps=[ev_mm[tl]])
                        pb_reader[tl] = s.sig("scalar")
                    (slot_d, d_wd) = ws.current()
                    ev_mmd = [None] * NT
                    for tl in range(NT):
                        deps = [pb_reader[tl], ev_hT[tl]]
                        if tl == 0:
                            deps.append(d_wd)
                        for dc in range(NDC):
                            s.op("tensor", lambda e, tl=tl, dc=dc, slot=slot_d: e.matmul(
                                out=pb[tl],
                                lhsT=vslice(ar1_bf[:], 0, 128, tl * 2048 + dc * 128, [[1, 128]]),
                                rhs=vslice(wbuf[slot][:], 0, 128, dc * 512, [[1, 512]]),
                                start=(dc == 0), stop=(dc == NDC - 1)),
                                deps=deps if dc == 0 else [])
                        ev_mmd[tl] = s.sig("tensor")
                        pe_tail = ev_mmd[tl]
                    ws.consumed(ev_mmd[3])
                    if oc < 3:
                        ws.prefetch(mk_loader_flat(oc + 1))
                    for tl in range(NT):
                        s.op("vector", lambda e, tl=tl, oc=oc, stg=stg: e.tensor_tensor(
                            out=vslice(stg[tl], 0, 128, oc * 512, [[1, 512]]),
                            in0=pb[tl],
                            in1=vslice(ar2_f[:], 0, 128, tl * 512, [[1, 512]]),
                            op=ALU.add), deps=[ev_mmd[tl]])
                        ev_ffn[tl][oc] = s.sig("vector")
                        pb_reader[tl] = ev_ffn[tl][oc]

                if dbg and seq == 0 and c == dbgc:
                    for _tl in range(NT):
                        tap(f"ffn{_tl}", stg[_tl], [128, 2048], F32, ev_ffn[_tl][3])
                ev_sto = [ln_stats_only(ffn_rows[tl], [ev_ffn[tl][q] for q in range(4)], tl)
                          for tl in range(NT)]
                ev_fin = [ln_fin(tl, ev_sto[tl]) for tl in range(NT)]
                for tl in range(NT):
                    g = c * NT + tl
                    t0 = g * 128
                    s.op("scalar", lambda e, tl=tl, fr=ffn_rows: e.activation(
                        out=fr[tl], in_=fr[tl], func=AF.Identity,
                        scale=stat[:, 2 * tl:2 * tl + 1], bias=stat[:, 2 * tl + 1:2 * tl + 2]),
                        deps=[ev_fin[tl]])
                    ev_lf = s.sig("scalar")
                    s.op("gpsimd", lambda e, tl=tl, xt=xt, fr=ffn_rows: e.tensor_tensor(
                        out=xt[tl], in0=xt[tl], in1=fr[tl], op=ALU.add), deps=[ev_lf])
                    ev_xf = s.sig("gpsimd")
                    ev_staging_last[tl] = ev_xf
                    ev_x_out_last[tl] = s.dma("sync", lambda e, tl=tl, seq=seq, t0=t0, xt=xt: e.dma_start(
                        out=out_d[seq, t0:t0 + 128, :], in_=xt[tl]), deps=[ev_xf])
                first = False
                first = False

        s.op("sync", lambda e: e.nop(), deps=[("qs", s.count["qs"]), ("qg", s.count["qg"])])

        block = st.enter_context(nc.Block())
        s.emit(block, sems)
    return nc


def _host_prep(inputs):
    wq = np.asarray(inputs["L0_attn_wq"])
    wk = np.asarray(inputs["L0_attn_wk"])
    wv = np.asarray(inputs["L0_attn_wv"])
    wo = np.asarray(inputs["L0_attn_wo"])
    perm_heads = []
    for j in range(4):
        for a in range(4):
            perm_heads += [8 * j + a, 8 * j + 4 + a]
    deint = np.concatenate([np.arange(0, 64, 2), np.arange(1, 64, 2)])
    wq_rows = np.concatenate([h * 64 + deint for h in perm_heads])
    wk_rows = np.concatenate([k * 64 + deint for k in range(8)])
    wo_cols = np.concatenate([np.arange(h * 64, h * 64 + 64) for h in perm_heads])
    i32 = np.arange(0, HD, 2, dtype=np.float32)
    theta = (1.0 / (10000.0 ** (i32 / np.float32(HD)))).astype(np.float32)
    t = np.arange(T, dtype=np.float32)
    freqs = t[:, None] * theta[None, :]
    cos = np.cos(freqs).astype(np.float32)
    sin = np.sin(freqs).astype(np.float32)
    return {
        "wq": np.ascontiguousarray(wq[wq_rows]),
        "wk": np.ascontiguousarray(wk[wk_rows]),
        "wv": np.ascontiguousarray(wv),
        "wo": np.ascontiguousarray(wo[:, wo_cols]),
        "up": np.ascontiguousarray(np.asarray(inputs["L0_ffn_W_up"])),
        "down": np.ascontiguousarray(np.asarray(inputs["L0_ffn_W_down"])),
        "expert": np.ascontiguousarray(np.asarray(inputs["L0_ffn_W_expert"])),
        "gate": np.ascontiguousarray(np.asarray(inputs["L0_ffn_W_gate"])),
        "cosq": cos * np.float32(0.125), "sinq": sin * np.float32(0.125),
        "cosk": cos, "sink": sin,
        "ident": np.eye(128, dtype=np.float32).astype(bfdt),
        "masktile": (np.arange(128)[None, :] >= np.arange(128)[:, None]).astype(bfdt),
    }


def kernel(**inputs):
    x = np.asarray(inputs["L0_attn_x"], dtype=np.float32)
    consts = _host_prep(inputs)
    nc = build_program()
    in_maps = []
    for core in range(8):
        m = dict(consts)
        m["x"] = np.ascontiguousarray(x[2 * core:2 * core + 2])
        in_maps.append(m)
    res = run_bass_kernel_spmd(nc, in_maps, core_ids=list(range(8)))
    out = np.concatenate([np.asarray(r["out"]) for r in res.results], axis=0)
    return out.reshape(B, T, D)
